# version 15
# baseline (speedup 1.0000x reference)
import numpy as np
from contextlib import ExitStack

import concourse.bass as bass
import concourse.mybir as mybir
import concourse.bass_utils as _bu
from concourse.bass_utils import run_bass_kernel_spmd

if not getattr(_bu, "_ldw_opt_patched", False):
    _orig_run_command = _bu.run_command

    def _run_command_ldw(cmd, **kw):
        cmd = [c.replace("--enable-ldw-opt=false", "--enable-ldw-opt=true")
               if isinstance(c, str) else c for c in cmd]
        return _orig_run_command(cmd, **kw)

    _bu.run_command = _run_command_ldw
    _bu._ldw_opt_patched = True

F32 = mybir.dt.float32
BF16 = mybir.dt.bfloat16
AF = mybir.ActivationFunctionType
ALU = mybir.AluOpType
AX = mybir.AxisListType

S = 2048
D = 1024
NT = S // 128
NCH = 8
NE = 16
EI = 512
SI = 1024
LN_EPS = 1e-5
NCORE = 8

ENGINES = ("pe", "ve", "se", "sy", "gp")


def _flat(waits):
    out = []
    for w in waits:
        if w is None:
            continue
        if isinstance(w, list):
            out.extend(x for x in w if x is not None)
        else:
            out.append(w)
    return out


class Sched:
    def __init__(self):
        self.ops = {e: [] for e in ENGINES}
        self.cnt = {}
        self.maxw = {e: {} for e in ENGINES}

    def sem(self, name):
        self.cnt[name] = 0

    def add(self, eng, fn, waits=(), inc=None, n=1):
        ws = []
        for w in _flat(waits):
            s, v = w
            if v <= 0 or v <= self.maxw[eng].get(s, 0):
                continue
            self.maxw[eng][s] = v
            ws.append((s, v))
        ev = None
        if inc is not None:
            self.cnt[inc] += n
            ev = (inc, self.cnt[inc])
        self.ops[eng].append((fn, tuple(ws), inc, n))
        return ev

    def pe(self, fn, waits=(), want=True):
        return self.add("pe", fn, waits, "PE" if want else None)

    def ve(self, fn, waits=()):
        return self.add("ve", fn, waits, "VE")

    def se(self, fn, waits=()):
        return self.add("se", fn, waits, "SE")

    def dma(self, eng, fn, waits=(), inc=None):
        return self.add(eng, fn, waits, inc, n=16)


def build_program():
    nc = bass.Bass("TRN2", target_bir_lowering=False)

    xin = nc.dram_tensor("x", [S, D], F32, kind="ExternalInput").ap()
    wq_h = nc.dram_tensor("Wq", [D, D], F32, kind="ExternalInput").ap()
    bq_h = nc.dram_tensor("bq", [D], F32, kind="ExternalInput").ap()
    wk_h = nc.dram_tensor("Wk", [D, D], F32, kind="ExternalInput").ap()
    bk_h = nc.dram_tensor("bk", [D], F32, kind="ExternalInput").ap()
    wv_h = nc.dram_tensor("Wv", [D, D], F32, kind="ExternalInput").ap()
    bv_h = nc.dram_tensor("bv", [D], F32, kind="ExternalInput").ap()
    wg_h = nc.dram_tensor("Wg", [D, D], F32, kind="ExternalInput").ap()
    bg_h = nc.dram_tensor("bg", [D], F32, kind="ExternalInput").ap()
    wo_h = nc.dram_tensor("Wo", [D, D], F32, kind="ExternalInput").ap()
    bo_h = nc.dram_tensor("bo", [D], F32, kind="ExternalInput").ap()
    gw_h = nc.dram_tensor("gate_w", [NE, D], F32, kind="ExternalInput").ap()
    w1_h = nc.dram_tensor("w1", [NE, D, EI], F32, kind="ExternalInput").ap()
    w2_h = nc.dram_tensor("w2", [NE, EI, D], F32, kind="ExternalInput").ap()
    w3_h = nc.dram_tensor("w3", [NE, D, EI], F32, kind="ExternalInput").ap()
    ws1_h = nc.dram_tensor("ws1", [D, SI], F32, kind="ExternalInput").ap()
    ws2_h = nc.dram_tensor("ws2", [SI, D], F32, kind="ExternalInput").ap()
    ws3_h = nc.dram_tensor("ws3", [D, SI], F32, kind="ExternalInput").ap()
    aux_h = nc.dram_tensor("aux", [128, 2304], F32, kind="ExternalInput").ap()
    out_h = nc.dram_tensor("out", [S, D], F32, kind="ExternalOutput").ap()
    h_scr = nc.dram_tensor("h_scr", [S, D], F32, kind="Internal").ap()
    hth_scr = nc.dram_tensor("hth_scr", [8, 128, S], BF16, kind="Internal").ap()
    htl_scr = nc.dram_tensor("htl_scr", [8, 128, S], BF16, kind="Internal").ap()

    Sd = Sched()
    for nm in ("PE", "VE", "SE", "GP", "GPD", "LDI", "LDB", "XA0", "XA1", "WA",
               "XB0", "XB1", "XBP0", "XBP1", "WB", "HW", "HTW", "HT", "HR",
               "OUT", "SWL", "WSH"):
        Sd.sem(nm)

    bfree = {i: None for i in range(8)}

    with ExitStack() as ctx:
        PW = ctx.enter_context(nc.sbuf_tensor("PW", [128, 26624], F32))
        PA = ctx.enter_context(nc.sbuf_tensor("PA", [128, 16384], F32))
        PM = ctx.enter_context(nc.sbuf_tensor("PM", [128, 5952], F32))
        PS = ctx.enter_context(nc.sbuf_tensor("PS", [128, 4096], F32))
        pb = [ctx.enter_context(nc.psum_tensor("pb%d" % i, [128, 512], F32))
              for i in range(8)]
        sems = {nm: ctx.enter_context(nc.semaphore(nm)) for nm in Sd.cnt}

        ident = PM[:, 0:128]
        bq_col = PM[:, 128:136]
        bg_col = PM[:, 136:144]
        bkb = PM[:, 144:1168]
        bvb = PM[:, 1168:2192]
        bob = PM[:, 2192:3216]
        onesr = PM[0:1, 3216:3344]
        kvbd = PM[:, 3344:4368]
        kvbd_hi = PM[:, 4368:4880].bitcast(BF16)
        kvbd_lo = PM[:, 4880:5392].bitcast(BF16)
        epsc = PM[:, 5936:5937]
        gwT_hi = PM[:, 5776:5840].bitcast(BF16)
        gwT_lo = PM[:, 5840:5904].bitcast(BF16)
        onehot = PS[:, 0:2048]

        brow = PS[0:1, 2048:3072]
        btmp = PS[0:8, 3328:3456]
        g16 = PS[0:16, 2048:3072]
        gwTf = PS[:, 3456:3584]

        wstage = PW[:, 24576:26624]

        Sd.dma("sy", lambda e: e.dma_start(out=ident, in_=aux_h[:, 0:128]), inc="LDI")
        Sd.dma("sy", lambda e: e.dma_start(out=onehot, in_=aux_h[:, 128:2176]), inc="LDI")
        Sd.dma("sy", lambda e: e.dma_start(out=PM[:, 3216:3344], in_=aux_h[:, 2176:2304]), inc="LDI")
        e_ldi = ("LDI", 48)

        ev_eps = Sd.ve(lambda e: e.memset(epsc, LN_EPS))

        xa_pre = [None, None]
        for t in range(2):
            xa_pre[t] = Sd.dma("sy", lambda e, t=t: e.dma_start(
                out=PA[:, t * 1024:(t + 1) * 1024], in_=xin[t * 128:(t + 1) * 128, :]),
                inc="XA%d" % t)

        def bias_bcast(b_hbm, dest):
            ev_l = Sd.dma("sy", lambda e, b=b_hbm: e.dma_start(out=brow, in_=b.unsqueeze(0)),
                          waits=[("PE", Sd.cnt["PE"])], inc="LDB")
            last = None
            for hf in range(2):
                ev_mm = Sd.pe(lambda e, hf=hf: e.matmul(
                    pb[2][:, :], onesr, brow[:, hf * 512:(hf + 1) * 512],
                    start=True, stop=True), waits=[ev_l, e_ldi, bfree[2]])
                last = Sd.ve(lambda e, hf=hf, d=dest: e.tensor_copy(
                    out=d[:, hf * 512:(hf + 1) * 512], in_=pb[2][:, :]), waits=[ev_mm])
                bfree[2] = last
            return last

        ev_bkb = bias_bcast(bk_h, bkb)
        ev_bvb = bias_bcast(bv_h, bvb)
        ev_bob = bias_bcast(bo_h, bob)

        def bias_col(b_hbm, dest):
            ev_l = Sd.dma("sy", lambda e, b=b_hbm: e.dma_start(
                out=btmp, in_=b.rearrange("(c p) -> c p", p=128)),
                waits=[("PE", Sd.cnt["PE"])], inc="LDB")
            ev_t = Sd.pe(lambda e: e.transpose(pb[3][:, 0:8], btmp, ident[0:8, 0:8]),
                         waits=[ev_l, e_ldi, bfree[3]])
            ev_e = Sd.ve(lambda e, d=dest: e.tensor_copy(out=d, in_=pb[3][:, 0:8]),
                         waits=[ev_t])
            bfree[3] = ev_e
            return ev_e

        ev_bq = bias_col(bq_h, bq_col)
        ev_bg = bias_col(bg_h, bg_col)

        ev_gwl = Sd.dma("sy", lambda e: e.dma_start(out=g16, in_=gw_h),
                        waits=[("PE", Sd.cnt["PE"])], inc="LDB")
        ev_gwf = None
        for c in range(8):
            ev_t = Sd.pe(lambda e, c=c: e.transpose(
                pb[3][:, 128:144], g16[0:16, c * 128:(c + 1) * 128], ident[0:16, 0:16]),
                waits=[ev_gwl, e_ldi, bfree[3]])
            ev_gwf = Sd.ve(lambda e, c=c: e.tensor_copy(
                out=gwTf[:, c * 16:(c + 1) * 16], in_=pb[3][:, 128:144]), waits=[ev_t])
            bfree[3] = ev_gwf
        ev_gwh = Sd.se(lambda e: e.copy(out=gwT_hi, in_=gwTf), waits=[ev_gwf])
        ev_gwT = Sd.ve(lambda e: e.tensor_tensor(out=gwT_lo, in0=gwTf, in1=gwT_hi,
                                                 op=ALU.subtract), waits=[ev_gwh])

        def wsplit(stage_ap, hi_ap, lo_ap, waits_se=(), waits_ve=()):
            ev_hi = Sd.se(lambda e: e.copy(out=hi_ap, in_=stage_ap), waits=list(waits_se))
            ev_lo = Sd.ve(lambda e: e.tensor_tensor(out=lo_ap, in0=stage_ap, in1=hi_ap,
                                                    op=ALU.subtract),
                          waits=[ev_hi] + list(waits_ve))
            return ev_hi, ev_lo

        wk_hi = PW[:, 0:4096].bitcast(BF16)
        wk_lo = PW[:, 4096:8192].bitcast(BF16)
        wv_hi = PW[:, 8192:12288].bitcast(BF16)
        wv_lo = PW[:, 12288:16384].bitcast(BF16)
        wa_ev = [[None] * 4, [None] * 4]
        stage_free = None
        for half in range(4):
            for wi, hbm in ((0, wk_h), (1, wv_h)):
                ev_l = Sd.dma("sy", lambda e, hbm=hbm, half=half: e.dma_start(
                    out=wstage.rearrange("p (c i) -> p c i", c=2),
                    in_=hbm[half * 256:(half + 1) * 256].rearrange("(c p) i -> p c i", p=128)),
                    waits=[stage_free], inc="WA")
                hi = (wk_hi, wv_hi)[wi]
                lo = (wk_lo, wv_lo)[wi]
                ev_hi, ev_lo = wsplit(
                    wstage, hi[:, half * 2048:(half + 1) * 2048],
                    lo[:, half * 2048:(half + 1) * 2048], waits_se=[ev_l])
                stage_free = ev_lo
                wa_ev[wi][half] = ev_lo

        def mm3s(bk_ap, sh, sl, mh, ml, first, last, waits0, want=True):
            Sd.pe(lambda e: e.matmul(bk_ap, sh, mh, start=first, stop=False),
                  waits=waits0, want=False)
            Sd.pe(lambda e: e.matmul(bk_ap, sh, ml, start=False, stop=False), want=False)
            return Sd.pe(lambda e: e.matmul(bk_ap, sl, mh, start=False, stop=last),
                         want=want)

        xa = [PA[:, 0:1024], PA[:, 1024:2048]]
        xT_hi = [PA[:, 2048:2560].bitcast(BF16), PA[:, 3072:3584].bitcast(BF16)]
        xT_lo = [PA[:, 2560:3072].bitcast(BF16), PA[:, 3584:4096].bitcast(BF16)]
        kkb = PA[:, 4096:5120]
        kk_hi = PA[:, 5120:5632].bitcast(BF16)
        kk_lo = PA[:, 5632:6144].bitcast(BF16)
        vvb = PA[:, 6144:7168]
        vv_hi = PA[:, 7168:7680].bitcast(BF16)
        vv_lo = PA[:, 7680:8192].bitcast(BF16)
        t1b = PA[:, 8192:9216]
        t2b = PA[:, 9216:10240]

        xa_free = [None, None]
        xa_ready = [None, None]
        xT_free = [None, None]
        kk_guard = None
        kv_last = None

        for t in range(NT):
            p = t % 2
            if t < 2:
                xa_ready[p] = xa_pre[t]
            else:
                xa_ready[p] = Sd.dma("sy", lambda e, t=t, p=p: e.dma_start(
                    out=xa[p], in_=xin[t * 128:(t + 1) * 128, :]),
                    waits=[xa_free[p]], inc="XA%d" % p)
            ev_xT = [None, None]
            for bi in range(2):
                ev_tr = None
                for j in range(4):
                    c = bi * 4 + j
                    ev_tr = Sd.pe(lambda e, c=c, p=p, bi=bi, j=j: e.transpose(
                        pb[bi][:, j * 128:(j + 1) * 128],
                        xa[p][:, c * 128:(c + 1) * 128], ident),
                        waits=[xa_ready[p], e_ldi, bfree[bi]], want=(j == 3))
                ev_hi = Sd.se(lambda e, p=p, bi=bi: e.copy(
                    out=xT_hi[p][:, bi * 512:(bi + 1) * 512], in_=pb[bi][:, :]),
                    waits=[ev_tr, xT_free[p]])
                ev_lo = Sd.ve(lambda e, p=p, bi=bi: e.tensor_tensor(
                    out=xT_lo[p][:, bi * 512:(bi + 1) * 512], in0=pb[bi][:, :],
                    in1=xT_hi[p][:, bi * 512:(bi + 1) * 512], op=ALU.subtract),
                    waits=[ev_hi, xT_free[p]])
                bfree[bi] = ev_lo
                ev_xT[bi] = ev_lo
            xa_free[p] = ev_tr

            def proj(whi, wlo, bank0, wi_, p=p, ev_xT=ev_xT):
                stops = []
                for hf in range(2):
                    ev = None
                    for c in range(8):
                        sh = xT_hi[p][:, c * 128:(c + 1) * 128]
                        sl = xT_lo[p][:, c * 128:(c + 1) * 128]
                        mh = whi[:, c * 1024 + hf * 512: c * 1024 + (hf + 1) * 512]
                        ml = wlo[:, c * 1024 + hf * 512: c * 1024 + (hf + 1) * 512]
                        w0 = [ev_xT[0], ev_xT[1], bfree[bank0 + hf], wa_ev[wi_][c // 2]]
                        ev = mm3s(pb[bank0 + hf][:, :], sh, sl, mh, ml,
                                  c == 0, c == 7, w0, want=(c == 7))
                    stops.append(ev)
                return stops

            k_stops = proj(wk_hi, wk_lo, 2, 0)
            v_stops = proj(wv_hi, wv_lo, 4, 1)
            xT_free[p] = v_stops[-1]

            ev_kpre = []
            for hf in range(2):
                ev_kpre.append(Sd.ve(lambda e, hf=hf: e.tensor_tensor(
                    out=kkb[:, hf * 512:(hf + 1) * 512], in0=pb[2 + hf][:, :],
                    in1=bkb[:, hf * 512:(hf + 1) * 512], op=ALU.add),
                    waits=[k_stops[hf], ev_bkb, kk_guard]))
                bfree[2 + hf] = ev_kpre[hf]
            ev_min = Sd.ve(lambda e: e.tensor_scalar(
                out=t1b, in0=kkb, scalar1=0.0, scalar2=None, op0=ALU.min),
                waits=list(ev_kpre))
            ev_exp = Sd.se(lambda e: e.activation(out=t2b, in_=t1b, func=AF.Exp),
                           waits=[ev_min])
            kk_ready = Sd.ve(lambda e: e.scalar_tensor_tensor(
                out=kkb, in0=kkb, scalar=0.0, in1=t2b, op0=ALU.max, op1=ALU.add),
                waits=[ev_exp])
            ev_kh = Sd.se(lambda e: e.copy(out=kk_hi, in_=kkb), waits=[kk_ready, kk_guard])
            ev_kl = Sd.ve(lambda e: e.tensor_tensor(out=kk_lo, in0=kkb, in1=kk_hi,
                                                    op=ALU.subtract), waits=[ev_kh])
            ev_vpre = []
            for hf in range(2):
                ev_vpre.append(Sd.ve(lambda e, hf=hf: e.tensor_tensor(
                    out=vvb[:, hf * 512:(hf + 1) * 512], in0=pb[4 + hf][:, :],
                    in1=bvb[:, hf * 512:(hf + 1) * 512], op=ALU.add),
                    waits=[v_stops[hf], ev_bvb, kk_guard]))
                bfree[4 + hf] = ev_vpre[hf]
            ev_vh = Sd.se(lambda e: e.copy(out=vv_hi, in_=vvb),
                          waits=[ev_vpre[0], ev_vpre[1], kk_guard])
            ev_vl = Sd.ve(lambda e: e.tensor_tensor(out=vv_lo, in0=vvb, in1=vv_hi,
                                                    op=ALU.subtract), waits=[ev_vh])

            for pr in range(8):
                co = (pr % 4) * 128
                bk_i = 6 + pr // 4
                sh = kk_hi[:, pr * 128:(pr + 1) * 128]
                sl = kk_lo[:, pr * 128:(pr + 1) * 128]
                mh = vv_hi[:, pr * 128:(pr + 1) * 128]
                ml = vv_lo[:, pr * 128:(pr + 1) * 128]
                last3 = (t == NT - 1 and pr % 4 == 3)
                first3 = (t == 0 and pr % 4 == 0)
                Sd.pe(lambda e, sh=sh, mh=mh, bk_i=bk_i, co=co, f3=first3: e.matmul(
                    pb[bk_i][:, co:co + 128], sh, mh, start=f3, stop=False),
                    waits=[ev_kl, ev_vl], want=False)
                Sd.pe(lambda e, sh=sh, ml=ml, bk_i=bk_i, co=co: e.matmul(
                    pb[bk_i][:, co:co + 128], sh, ml, start=False, stop=False), want=False)
                kv_last = Sd.pe(lambda e, sl=sl, mh=mh, bk_i=bk_i, co=co, l3=last3: e.matmul(
                    pb[bk_i][:, co:co + 128], sl, mh, start=False, stop=l3),
                    want=(pr == 7))
            kk_guard = kv_last

        a_ve_tail = ("VE", Sd.cnt["VE"])
        ev_z = Sd.ve(lambda e: e.memset(kvbd, 0.0))
        kv_f32 = None
        for pr in range(8):
            for hv in range(2):
                lo = hv * 64
                kv_f32 = Sd.ve(lambda e, pr=pr, lo=lo: e.tensor_copy(
                    out=kvbd[lo:lo + 64, pr * 128 + lo: pr * 128 + lo + 64],
                    in_=pb[6 + pr // 4][lo:lo + 64, (pr % 4) * 128 + lo:(pr % 4) * 128 + lo + 64]),
                    waits=[kv_last, ev_z])
        ev_kvh = Sd.se(lambda e: e.copy(out=kvbd_hi, in_=kvbd), waits=[kv_f32])
        kv_ev = Sd.ve(lambda e: e.tensor_tensor(out=kvbd_lo, in0=kvbd, in1=kvbd_hi,
                                                op=ALU.subtract), waits=[ev_kvh])
        bfree[6] = kv_f32
        bfree[7] = kv_f32

        barA_pe = ("PE", Sd.cnt["PE"])
        barA = [barA_pe, ("VE", Sd.cnt["VE"]), ("SE", Sd.cnt["SE"])]

        wq_hi = PW[:, 0:4096].bitcast(BF16)
        wq_lo = PW[:, 4096:8192].bitcast(BF16)
        wg_hi = PW[:, 8192:12288].bitcast(BF16)
        wg_lo = PW[:, 12288:16384].bitcast(BF16)
        wo_hi = PW[:, 16384:20480].bitcast(BF16)
        wo_lo = PW[:, 20480:24576].bitcast(BF16)
        wb_ev = [[None] * 4, [None] * 4, [None] * 4]
        stage_free_b = [stage_free] if stage_free else []
        for wi, hbm in ((0, wq_h), (1, wg_h), (2, wo_h)):
            hi = (wq_hi, wg_hi, wo_hi)[wi]
            lo = (wq_lo, wg_lo, wo_lo)[wi]
            for half in range(4):
                ev_l = Sd.dma("sy", lambda e, hbm=hbm, half=half: e.dma_start(
                    out=wstage.rearrange("p (c i) -> p c i", c=2),
                    in_=hbm[half * 256:(half + 1) * 256].rearrange("(c p) i -> p c i", p=128)),
                    waits=(stage_free_b if isinstance(stage_free_b, list) else [stage_free_b]),
                    inc="WB")
                hi_ap = hi[:, half * 2048:(half + 1) * 2048]
                lo_ap = lo[:, half * 2048:(half + 1) * 2048]
                ev_hi = Sd.se(lambda e, hi_ap=hi_ap: e.copy(out=hi_ap, in_=wstage),
                              waits=[ev_l, barA_pe] if (wi == 0 and half == 0) else [ev_l])
                ev_lo = Sd.add("gp", lambda e, hi_ap=hi_ap, lo_ap=lo_ap: e.tensor_tensor(
                    out=lo_ap, in0=wstage, in1=hi_ap, op=ALU.subtract),
                    waits=[ev_hi], inc="GP")
                stage_free_b = ev_lo
                wb_ev[wi][half] = ev_lo

        xb = [PA[:, 0:2048], PA[:, 2048:4096]]
        xTb_hi = PA[:, 4096:5120].bitcast(BF16)
        xTb_lo = PA[:, 5120:6144].bitcast(BF16)
        qtf = PA[:, 6144:6656]
        qT_hi = PA[:, 6656:7680].bitcast(BF16)
        qT_lo = PA[:, 7680:8704].bitcast(BF16)
        gsb = PA[:, 8704:10752]
        gaf = PA[:, 10752:11264]
        ga_hi = PA[:, 11264:12288].bitcast(BF16)
        ga_lo = PA[:, 12288:13312].bitcast(BF16)
        res1 = PA[:, 13312:14336]
        t1c = PA[:, 14336:14848]
        t2c = PA[:, 14848:15360]
        sqd = PA[:, 15360:16384]
        hTb_hi = PS[:, 2048:3072].bitcast(BF16)
        hTb_lo = PS[:, 3072:4096].bitcast(BF16)
        stat = PM[:, 5904:5920]
        ssum = stat[:, 0:1]
        negm = stat[:, 1:2]
        ss2 = stat[:, 2:3]
        m2v = stat[:, 3:4]
        varv = stat[:, 4:5]
        svv = stat[:, 5:6]
        rstd = stat[:, 6:7]

        xb_free = [[barA_pe, a_ve_tail], [barA_pe, a_ve_tail]]
        xb_ready = [None, None]
        xb_pre = [None, None]
        for cc0 in range(2):
            ev = None
            for st2 in range(2):
                ev = Sd.dma("gp", lambda e, cc0=cc0, st2=st2: e.dma_start(
                    out=xb[cc0][:, st2 * 1024:(st2 + 1) * 1024],
                    in_=xin[cc0 * 256 + st2 * 128: cc0 * 256 + (st2 + 1) * 128, :]),
                    waits=[barA_pe, a_ve_tail], inc="XBP%d" % cc0)
            xb_pre[cc0] = ev
        xTb_cons = None
        qT_cons = None
        ga_cons = None
        gs_cons = None
        qtf_free = [None, None]
        gaf_free = [None, None]
        sq_guard = []
        hTb_guard = None

        for cc in range(NCH):
            p = cc % 2
            if cc < 2:
                xb_ready[p] = xb_pre[cc]
            else:
                ev = None
                for st2 in range(2):
                    ev = Sd.dma("sy", lambda e, cc=cc, st2=st2, p=p: e.dma_start(
                        out=xb[p][:, st2 * 1024:(st2 + 1) * 1024],
                        in_=xin[cc * 256 + st2 * 128: cc * 256 + (st2 + 1) * 128, :]),
                        waits=[xb_free[p]], inc="XB%d" % p)
                xb_ready[p] = ev

            ev_xtev = []
            for wv_i in range(4):
                st2, cg = wv_i // 2, wv_i % 2
                bi = wv_i % 2
                ev_tr = None
                for j in range(4):
                    c = cg * 4 + j
                    ev_tr = Sd.pe(lambda e, st2=st2, c=c, p=p, bi=bi, j=j: e.transpose(
                        pb[bi][:, j * 128:(j + 1) * 128],
                        xb[p][:, st2 * 1024 + c * 128: st2 * 1024 + (c + 1) * 128], ident),
                        waits=[xb_ready[p], bfree[bi]], want=(j == 3))
                hi3 = xTb_hi.rearrange("q (c t) -> q c t", c=8)[:, cg * 4:(cg + 1) * 4, st2 * 128:(st2 + 1) * 128]
                lo3 = xTb_lo.rearrange("q (c t) -> q c t", c=8)[:, cg * 4:(cg + 1) * 4, st2 * 128:(st2 + 1) * 128]
                ps3 = pb[bi].rearrange("q (j t) -> q j t", j=4)[:, :, :]
                ev_hi = Sd.se(lambda e, hi3=hi3, ps3=ps3: e.copy(out=hi3, in_=ps3),
                              waits=[ev_tr, xTb_cons])
                ev_lo = Sd.ve(lambda e, lo3=lo3, ps3=ps3, hi3=hi3: e.tensor_tensor(
                    out=lo3, in0=ps3, in1=hi3, op=ALU.subtract),
                    waits=[ev_hi, xTb_cons])
                bfree[bi] = ev_lo
                ev_xtev.append(ev_lo)

            qT_ready = []
            for ec in range(8):
                bk_i = 2 + ec % 2
                par = ec % 2
                ev_q = None
                for c in range(8):
                    ev_q = mm3s(pb[bk_i][:, 0:256],
                                wq_hi[:, c * 1024 + ec * 128: c * 1024 + ec * 128 + 128],
                                wq_lo[:, c * 1024 + ec * 128: c * 1024 + ec * 128 + 128],
                                xTb_hi[:, c * 256:(c + 1) * 256],
                                xTb_lo[:, c * 256:(c + 1) * 256],
                                c == 0, c == 7,
                                ev_xtev + [wb_ev[0][c // 2], bfree[bk_i]], want=(c == 7))
                ev_t1 = Sd.ve(lambda e, ec=ec, par=par, bk_i=bk_i: e.tensor_scalar(
                    out=t1c[:, par * 256:(par + 1) * 256], in0=pb[bk_i][:, 0:256],
                    scalar1=bq_col[:, ec:ec + 1], scalar2=0.0, op0=ALU.add, op1=ALU.min),
                    waits=[ev_q, ev_bq, qtf_free[par]])
                ev_t2 = Sd.se(lambda e, par=par: e.activation(
                    out=t2c[:, par * 256:(par + 1) * 256], in_=t1c[:, par * 256:(par + 1) * 256],
                    func=AF.Exp), waits=[ev_t1])
                ev_t3 = Sd.ve(lambda e, ec=ec, par=par, bk_i=bk_i: e.tensor_scalar(
                    out=qtf[:, par * 256:(par + 1) * 256], in0=pb[bk_i][:, 0:256],
                    scalar1=bq_col[:, ec:ec + 1], scalar2=0.0, op0=ALU.add, op1=ALU.max),
                    waits=[ev_q, qtf_free[par]])
                bfree[bk_i] = ev_t3
                ev_q4 = Sd.ve(lambda e, par=par: e.tensor_tensor(
                    out=qtf[:, par * 256:(par + 1) * 256], in0=qtf[:, par * 256:(par + 1) * 256],
                    in1=t2c[:, par * 256:(par + 1) * 256], op=ALU.add),
                    waits=[ev_t2, ev_t3])
                ev_qh = Sd.se(lambda e, ec=ec, par=par: e.copy(
                    out=qT_hi[:, ec * 256:(ec + 1) * 256], in_=qtf[:, par * 256:(par + 1) * 256]),
                    waits=[ev_q4, qT_cons])
                ev_ql = Sd.ve(lambda e, ec=ec, par=par: e.tensor_tensor(
                    out=qT_lo[:, ec * 256:(ec + 1) * 256], in0=qtf[:, par * 256:(par + 1) * 256],
                    in1=qT_hi[:, ec * 256:(ec + 1) * 256], op=ALU.subtract),
                    waits=[ev_qh, qT_cons])
                qtf_free[par] = ev_ql
                qT_ready.append(ev_ql)

            gs_ready = []
            last_g_mm = None
            for ec in range(8):
                bk_i = 4 + ec % 2
                for c in range(8):
                    last_g_mm = mm3s(pb[bk_i][:, 0:256],
                                     wg_hi[:, c * 1024 + ec * 128: c * 1024 + ec * 128 + 128],
                                     wg_lo[:, c * 1024 + ec * 128: c * 1024 + ec * 128 + 128],
                                     xTb_hi[:, c * 256:(c + 1) * 256],
                                     xTb_lo[:, c * 256:(c + 1) * 256],
                                     c == 0, c == 7,
                                     ev_xtev + [wb_ev[1][c // 2], bfree[bk_i]], want=(c == 7))
                ev_s = Sd.se(lambda e, ec=ec, bk_i=bk_i: e.activation(
                    out=gsb[:, ec * 256:(ec + 1) * 256], in_=pb[bk_i][:, 0:256],
                    func=AF.Sigmoid, bias=bg_col[:, ec:ec + 1]),
                    waits=[last_g_mm, ev_bg, gs_cons])
                bfree[bk_i] = ev_s
                gs_ready.append(ev_s)
            xTb_cons = last_g_mm

            ga_ready = []
            for pr in range(8):
                bk_i = 6 + pr % 2
                par = pr % 2
                ev_a = mm3s(pb[bk_i][:, 0:256],
                            kvbd_hi[:, pr * 128:(pr + 1) * 128],
                            kvbd_lo[:, pr * 128:(pr + 1) * 128],
                            qT_hi[:, pr * 256:(pr + 1) * 256],
                            qT_lo[:, pr * 256:(pr + 1) * 256],
                            True, True,
                            [qT_ready[pr], kv_ev, bfree[bk_i]])
                ev_m = Sd.ve(lambda e, pr=pr, par=par, bk_i=bk_i: e.tensor_tensor(
                    out=gaf[:, par * 256:(par + 1) * 256], in0=gsb[:, pr * 256:(pr + 1) * 256],
                    in1=pb[bk_i][:, 0:256], op=ALU.mult),
                    waits=[ev_a, gs_ready[pr], gaf_free[par]])
                ev_gh = Sd.se(lambda e, pr=pr, par=par: e.copy(
                    out=ga_hi[:, pr * 256:(pr + 1) * 256], in_=gaf[:, par * 256:(par + 1) * 256]),
                    waits=[ev_m, ga_cons])
                ev_gl = Sd.ve(lambda e, pr=pr, par=par: e.tensor_tensor(
                    out=ga_lo[:, pr * 256:(pr + 1) * 256], in0=gaf[:, par * 256:(par + 1) * 256],
                    in1=ga_hi[:, pr * 256:(pr + 1) * 256], op=ALU.subtract),
                    waits=[ev_gh, ga_cons])
                bfree[bk_i] = ev_gl
                gaf_free[par] = ev_gl
                ga_ready.append(ev_gl)
            qT_cons = ga_ready[-1]

            for t128 in range(2):
                ob = [2 + t128 * 2, 3 + t128 * 2]
                o_stops = []
                for hf in range(2):
                    ev_o = None
                    for c in range(8):
                        ev_o = mm3s(pb[ob[hf]][:, :],
                                    ga_hi[:, c * 256 + t128 * 128: c * 256 + t128 * 128 + 128],
                                    ga_lo[:, c * 256 + t128 * 128: c * 256 + t128 * 128 + 128],
                                    wo_hi[:, c * 1024 + hf * 512: c * 1024 + (hf + 1) * 512],
                                    wo_lo[:, c * 1024 + hf * 512: c * 1024 + (hf + 1) * 512],
                                    c == 0, c == 7,
                                    ga_ready + [wb_ev[2][c // 2], bfree[ob[hf]]], want=(c == 7))
                    o_stops.append(ev_o)
                ga_cons = o_stops[-1]
                gs_cons = o_stops[-1]
                ev_r = []
                for hf in range(2):
                    ev_r.append(Sd.ve(lambda e, hf=hf, ob=ob: e.scalar_tensor_tensor(
                        out=res1[:, hf * 512:(hf + 1) * 512], in0=pb[ob[hf]][:, :],
                        scalar=1.0, in1=bob[:, hf * 512:(hf + 1) * 512],
                        op0=ALU.mult, op1=ALU.add),
                        waits=[o_stops[hf], ev_bob] + sq_guard))
                    bfree[ob[hf]] = ev_r[hf]
                ev_r2 = Sd.ve(lambda e, p=p, t128=t128: e.tensor_tensor(
                    out=res1, in0=res1, in1=xb[p][:, t128 * 1024:(t128 + 1) * 1024],
                    op=ALU.add), waits=ev_r + [xb_ready[p]])
                if t128 == 1:
                    xb_free[p] = [ev_r2]
                ev_s1 = Sd.ve(lambda e: e.reduce_sum(out=ssum, in_=res1, axis=AX.X),
                              waits=[ev_r2])
                ev_nm = Sd.ve(lambda e: e.tensor_scalar_mul(out=negm, in0=ssum, scalar1=-1.0 / D),
                              waits=[ev_s1])
                ev_sq = Sd.se(lambda e: e.activation(out=sqd, in_=res1, func=AF.Square,
                                                     accum_out=ss2),
                              waits=[ev_r2] + sq_guard)
                ev_m2 = Sd.ve(lambda e: e.tensor_tensor(out=m2v, in0=negm, in1=negm, op=ALU.mult),
                              waits=[ev_nm])
                ev_var = Sd.ve(lambda e: e.tensor_scalar(
                    out=varv, in0=ss2, scalar1=1.0 / D, scalar2=m2v, op0=ALU.mult,
                    op1=ALU.subtract), waits=[ev_sq, ev_m2])
                ev_sv = Sd.se(lambda e: e.activation(out=svv, in_=varv, func=AF.Sqrt,
                                                     bias=epsc), waits=[ev_var, ev_eps])
                ev_rs = Sd.ve(lambda e: e.reciprocal(out=rstd, in_=svv), waits=[ev_sv])
                ev_h = Sd.ve(lambda e: e.tensor_scalar(
                    out=sqd, in0=res1, scalar1=negm, scalar2=rstd, op0=ALU.add, op1=ALU.mult),
                    waits=[ev_rs, ev_sq])
                Sd.dma("gp", lambda e, cc=cc, t128=t128: e.dma_start(
                    out=h_scr[cc * 256 + t128 * 128: cc * 256 + (t128 + 1) * 128, :],
                    in_=sqd), waits=[ev_h], inc="HW")
                hw_now = ("HW", Sd.cnt["HW"])
                htr_evs = []
                for cg in range(2):
                    bi = cg
                    ev_tr = None
                    for j in range(4):
                        c = cg * 4 + j
                        ev_tr = Sd.pe(lambda e, c=c, j=j, bi=bi: e.transpose(
                            pb[bi][:, j * 128:(j + 1) * 128],
                            sqd[:, c * 128:(c + 1) * 128], ident),
                            waits=[ev_h, bfree[bi]], want=(j == 3))
                    hi3 = hTb_hi.rearrange("q (c t) -> q c t", c=8)[:, cg * 4:(cg + 1) * 4, t128 * 128:(t128 + 1) * 128]
                    lo3 = hTb_lo.rearrange("q (c t) -> q c t", c=8)[:, cg * 4:(cg + 1) * 4, t128 * 128:(t128 + 1) * 128]
                    ps3 = pb[bi].rearrange("q (j t) -> q j t", j=4)[:, :, :]
                    ev_hhi = Sd.se(lambda e, hi3=hi3, ps3=ps3: e.copy(out=hi3, in_=ps3),
                                   waits=[ev_tr, hTb_guard])
                    ev_hlo = Sd.ve(lambda e, lo3=lo3, ps3=ps3, hi3=hi3: e.tensor_tensor(
                        out=lo3, in0=ps3, in1=hi3, op=ALU.subtract),
                        waits=[ev_hhi, hTb_guard])
                    bfree[bi] = ev_hlo
                    htr_evs.append(ev_hlo)
                sq_guard = [htr_evs[-1], hw_now]
            for c in range(8):
                Sd.dma("gp", lambda e, c=c, cc=cc: e.dma_start(
                    out=hth_scr[c, :, cc * 256:(cc + 1) * 256],
                    in_=hTb_hi[:, c * 256:(c + 1) * 256]),
                    waits=[sq_guard[0]], inc="HTW")
                hTb_guard = Sd.dma("gp", lambda e, c=c, cc=cc: e.dma_start(
                    out=htl_scr[c, :, cc * 256:(cc + 1) * 256],
                    in_=hTb_lo[:, c * 256:(c + 1) * 256]),
                    inc="HTW")

        barB = [("PE", Sd.cnt["PE"]), ("VE", Sd.cnt["VE"]), ("SE", Sd.cnt["SE"]),
                ("HW", Sd.cnt["HW"]), ("HTW", Sd.cnt["HTW"])]
        hw_total = ("HW", Sd.cnt["HW"])

        hTm_hi = PA[:, 0:4096].bitcast(BF16)
        hTm_lo = PA[:, 4096:8192].bitcast(BF16)
        acc = PA[:, 8192:16384]
        cbuf = [PM[:, 144:1168], PM[:, 1168:2192]]
        aTf = PM[:, 2192:2704]
        sh1b = PM[:, 2704:3216]
        aT_hi = PM[:, 3216:4240].bitcast(BF16)
        aT_lo = PM[:, 4240:5264].bitcast(BF16)
        h3cb = PM[:, 5264:5776]
        res2t = PM[:, 144:1168]
        sqd2 = PM[:, 1168:2192]
        outt = PM[:, 2192:3216]
        hln = PS[:, 2048:3072]
        cT = PS[0:16, 3072:4096]
        csc = PM[:, 5264:5392]
        eLb = PM[:, 5392:5408]
        gpad = PM[:, 5408:5416]
        g8b = PM[:, 5416:5424]
        gm4 = PM[:, 5424:5428]
        gex = PM[:, 5428:5444]
        mskb = PM[:, 5444:5460]
        m8b = PM[:, 5460:5468]
        selb = PM[:, 5468:5484]
        mst = PM[:, 5920:5936]
        mxs = mst[:, 0:1]
        nmx = mst[:, 1:2]
        sums = mst[:, 2:3]
        rs_ = mst[:, 3:4]
        ssum2 = mst[:, 4:5]
        negm2 = mst[:, 5:6]
        ss22 = mst[:, 6:7]
        m22 = mst[:, 7:8]
        var2 = mst[:, 8:9]
        sv2 = mst[:, 9:10]
        rstd2 = mst[:, 10:11]

        def slot_hi(kind, par):
            base = kind * 8192 + par * 4096
            return PW[:, base:base + 2048].bitcast(BF16)

        def slot_lo(kind, par):
            base = kind * 8192 + par * 4096
            return PW[:, base + 2048:base + 4096].bitcast(BF16)

        shared_hi = [PW[:, k * 8192:k * 8192 + 4096].bitcast(BF16) for k in range(3)]
        shared_lo = [PW[:, k * 8192 + 4096:(k + 1) * 8192].bitcast(BF16) for k in range(3)]

        half_end_bar = list(barB)
        sh_h_last = None
        hln_free = None
        res2_guard = None
        out_guard = None
        sh1_guard = None
        h3c_guard = None
        aTf_guard = None
        aT_cons = None
        slot_free = [[barB[0], barB[0]], [barB[0], barB[0]], [barB[0], barB[0]]]
        slot_rdy = [[None, None], [None, None], [None, None]]
        cb_ready = [None, None]
        stage_free_m = stage_free_b if isinstance(stage_free_b, list) else [stage_free_b]

        def load_split_slot(kind, par, e_idx):
            nonlocal stage_free_m
            hbm = (w1_h, w3_h, w2_h)[kind]
            rows = 512 if kind != 2 else 256
            nchunk = 4 if kind != 2 else 2
            ev_lo = None
            for half in range(2):
                ev_l = Sd.dma("sy", lambda e, hbm=hbm, e_idx=e_idx, rows=rows, nchunk=nchunk, half=half: e.dma_start(
                    out=wstage.rearrange("p (c i) -> p c i", c=nchunk),
                    in_=hbm[e_idx][half * rows:(half + 1) * rows].rearrange("(c p) i -> p c i", p=128)),
                    waits=stage_free_m, inc="SWL")
                hi_ap = slot_hi(kind, par)[:, half * 2048:(half + 1) * 2048]
                lo_ap = slot_lo(kind, par)[:, half * 2048:(half + 1) * 2048]
                ev_hi = Sd.dma("gp", lambda e, hi_ap=hi_ap: e.dma_start(
                    out=hi_ap, in_=wstage), waits=[ev_l, slot_free[kind][par]], inc="GPD")
                ev_lo = Sd.add("gp", lambda e, hi_ap=hi_ap, lo_ap=lo_ap: e.tensor_tensor(
                    out=lo_ap, in0=wstage, in1=hi_ap, op=ALU.subtract),
                    waits=[ev_hi], inc="GP")
                stage_free_m = [ev_lo]
            return ev_lo

        for hh in range(2):
            pe_gate = list(barB) if hh == 0 else [sh_h_last]
            ve_gate = list(barB) if hh == 0 else [("PE", Sd.cnt["PE"]), ("VE", Sd.cnt["VE"])]
            out_gate = None if hh == 0 else ("OUT", Sd.cnt["OUT"])
            ev_ht = None
            for c in range(8):
                Sd.dma("sy", lambda e, c=c, hh=hh: e.dma_start(
                    out=hTm_hi[:, c * 1024:(c + 1) * 1024],
                    in_=hth_scr[c, :, hh * 1024:(hh + 1) * 1024]),
                    waits=pe_gate, inc="HT")
                ev_ht = Sd.dma("sy", lambda e, c=c, hh=hh: e.dma_start(
                    out=hTm_lo[:, c * 1024:(c + 1) * 1024],
                    in_=htl_scr[c, :, hh * 1024:(hh + 1) * 1024]),
                    inc="HT")
            ev_acc0 = Sd.ve(lambda e: e.memset(acc, 0.0), waits=ve_gate)

            Lbuf = PM[:, 5264:5392]
            Ebuf = PM[:, 5392:5520]
            Mbuf = PS[:, 3584:3712]
            Mb2 = PS[:, 3712:3840]
            EQ128 = PS[:, 3840:3968]
            EQ32 = PS[:, 3968:4000]
            g32 = PS[:, 4000:4032]
            mx8 = PS[:, 4032:4040]
            m1b = PS[:, 4040:4048]
            thr2b = PS[:, 4048:4056]
            sumsb = PS[:, 4056:4064]
            rsb = PS[:, 4064:4072]
            thr4b = PS[:, 4072:4080]
            BIGC = 1e30

            ev_Ls = []
            for st in range(8):
                lb = 0 if st % 2 == 0 else 2
                ev_l = None
                for c in range(8):
                    sh = hTm_hi[:, c * 1024 + st * 128: c * 1024 + st * 128 + 128]
                    sl = hTm_lo[:, c * 1024 + st * 128: c * 1024 + st * 128 + 128]
                    gh_ = gwT_hi[:, c * 16:(c + 1) * 16]
                    gl_ = gwT_lo[:, c * 16:(c + 1) * 16]
                    w0 = [ev_ht, ev_gwT, bfree[lb]] + (pe_gate if st == 0 else [])
                    Sd.pe(lambda e, sh=sh, lb=lb, c=c, gh_=gh_: e.matmul(
                        pb[lb][:, 0:16], sh, gh_, start=(c == 0), stop=False), waits=w0,
                        want=False)
                    Sd.pe(lambda e, sh=sh, lb=lb, gl_=gl_: e.matmul(
                        pb[lb][:, 0:16], sh, gl_, start=False, stop=False), want=False)
                    Sd.pe(lambda e, sl=sl, lb=lb, gh_=gh_: e.matmul(
                        pb[lb][:, 0:16], sl, gh_, start=False, stop=False), want=False)
                    ev_l = Sd.pe(lambda e, sl=sl, lb=lb, c=c, gl_=gl_: e.matmul(
                        pb[lb][:, 0:16], sl, gl_, start=False, stop=(c == 7)),
                        want=(c == 7))
                ev_L = Sd.ve(lambda e, st=st, lb=lb: e.tensor_copy(
                    out=Lbuf[:, st * 16:(st + 1) * 16], in_=pb[lb][:, 0:16]),
                    waits=[ev_l] + (ve_gate if st == 0 else []))
                bfree[lb] = ev_L
                ev_Ls.append(ev_L)

            L3 = Lbuf.rearrange("p (s e) -> p s e", s=8)
            E3 = Ebuf.rearrange("p (s e) -> p s e", s=8)
            ev1 = Sd.ve(lambda e: e.tensor_reduce(out=mx8, in_=L3, axis=AX.X, op=ALU.max),
                        waits=ev_Ls)
            ev2 = Sd.ve(lambda e: e.tensor_tensor(
                out=E3, in0=L3,
                in1=mx8.unsqueeze(2).to_broadcast([128, 8, 16]),
                op=ALU.subtract), waits=[ev1])
            ev3 = Sd.se(lambda e: e.activation(out=Lbuf, in_=Ebuf, func=AF.Exp),
                        waits=[ev2])
            ev4 = Sd.ve(lambda e: e.tensor_reduce(out=sumsb, in_=L3, axis=AX.X, op=ALU.add),
                        waits=[ev3])
            ev5 = Sd.ve(lambda e: e.reciprocal(out=rsb, in_=sumsb), waits=[ev4])
            ev6 = Sd.ve(lambda e: e.tensor_tensor(
                out=E3, in0=L3,
                in1=rsb.unsqueeze(2).to_broadcast([128, 8, 16]),
                op=ALU.mult), waits=[ev5, ev3])
            Eg = Ebuf.rearrange("p (g j) -> p g j", j=4)
            ev7 = Sd.ve(lambda e: e.tensor_reduce(out=g32, in_=Eg, axis=AX.X, op=ALU.max),
                        waits=[ev6])
            G3 = g32.rearrange("p (s g) -> p s g", s=8)
            ev8 = Sd.ve(lambda e: e.tensor_reduce(out=m1b, in_=G3, axis=AX.X, op=ALU.max),
                        waits=[ev7])
            G32v = EQ32.rearrange("p (s g) -> p s g", s=8)
            ev9 = Sd.ve(lambda e: e.tensor_tensor(
                out=G32v, in0=G3,
                in1=m1b.unsqueeze(2).to_broadcast([128, 8, 4]),
                op=ALU.is_equal), waits=[ev8])
            ev10 = Sd.ve(lambda e: e.scalar_tensor_tensor(
                out=EQ32, in0=EQ32, scalar=-BIGC, in1=g32, op0=ALU.mult, op1=ALU.add),
                waits=[ev9])
            ev11 = Sd.ve(lambda e: e.tensor_reduce(
                out=thr2b, in_=EQ32.rearrange("p (s g) -> p s g", s=8), axis=AX.X,
                op=ALU.max), waits=[ev10])
            ev12 = Sd.ve(lambda e: e.tensor_tensor(
                out=G32v, in0=G3,
                in1=thr2b.unsqueeze(2).to_broadcast([128, 8, 4]),
                op=ALU.is_ge), waits=[ev11])
            Mg = Mbuf.rearrange("p (g j) -> p g j", j=4)
            ev13 = Sd.ve(lambda e: e.tensor_tensor(
                out=Mg, in0=Eg,
                in1=EQ32.unsqueeze(2).to_broadcast([128, 32, 4]),
                op=ALU.mult), waits=[ev12, ev6])
            ev14 = Sd.ve(lambda e: e.tensor_copy(out=Mb2, in_=Mbuf), waits=[ev13])
            M3 = Mb2.rearrange("p (s e) -> p s e", s=8)
            evp = ev14
            for r in range(4):
                evm = Sd.ve(lambda e: e.tensor_reduce(out=thr4b, in_=M3, axis=AX.X,
                                                      op=ALU.max), waits=[evp])
                if r == 3:
                    evp = evm
                    break
                eve = Sd.ve(lambda e: e.tensor_tensor(
                    out=EQ128.rearrange("p (s e) -> p s e", s=8), in0=M3,
                    in1=thr4b.unsqueeze(2).to_broadcast([128, 8, 16]),
                    op=ALU.is_equal), waits=[evm])
                evp = Sd.ve(lambda e: e.scalar_tensor_tensor(
                    out=Mb2, in0=EQ128, scalar=-BIGC, in1=Mb2, op0=ALU.mult, op1=ALU.add),
                    waits=[eve])
            ev15 = Sd.ve(lambda e: e.tensor_tensor(
                out=EQ128.rearrange("p (s e) -> p s e", s=8),
                in0=Mbuf.rearrange("p (s e) -> p s e", s=8),
                in1=thr4b.unsqueeze(2).to_broadcast([128, 8, 16]),
                op=ALU.is_ge), waits=[evp, ev13])
            ev_c = Sd.ve(lambda e: e.tensor_tensor(out=csc, in0=Ebuf, in1=EQ128,
                                                   op=ALU.mult), waits=[ev15, ev6])

            cT_evs = []
            for st in range(8):
                ev15t = Sd.pe(lambda e, st=st: e.transpose(
                    pb[1][0:16, 0:128], csc[:, st * 16:(st + 1) * 16], ident),
                    waits=[ev_c, bfree[1]])
                ev16t = Sd.ve(lambda e, st=st: e.tensor_copy(
                    out=cT[:, st * 128:(st + 1) * 128], in_=pb[1][0:16, 0:128]),
                    waits=[ev15t] + (ve_gate if st == 0 else []))
                bfree[1] = ev16t
                cT_evs.append(ev16t)

            for e_i in range(NE):
                par = e_i % 2
                for kind in range(3):
                    slot_rdy[kind][par] = load_split_slot(kind, par, e_i)
                cb_mm = []
                for hf in range(2):
                    cbk = 4 + hf
                    cb_mm.append(Sd.pe(lambda e, e_i=e_i, hf=hf, cbk=cbk: e.matmul(
                        pb[cbk][:, :], onehot[0:16, e_i * 128:(e_i + 1) * 128],
                        cT[:, hf * 512:(hf + 1) * 512], start=True, stop=True),
                        waits=cT_evs + [e_ldi, bfree[cbk]]))
                ev_cb = None
                for hf in range(2):
                    ev_cb = Sd.ve(lambda e, par=par, hf=hf, cbk=4 + hf: e.tensor_copy(
                        out=cbuf[par][:, hf * 512:(hf + 1) * 512], in_=pb[cbk][:, :]),
                        waits=[cb_mm[hf]])
                    bfree[4 + hf] = ev_cb
                cb_ready[par] = ev_cb

                h1_last = None
                h3_last = None
                for tg in range(2):
                    a_evs = []
                    y_stop = {}
                    YB = {0: 2, 1: 4, 2: 6, 3: 2}

                    def y_partial(j, t128s, tg=tg, par=par, a_evs=a_evs, y_stop=y_stop, YB=YB):
                        for t128 in t128s:
                            for hf in range(2):
                                bidx = YB[t128] + hf
                                sh = aT_hi[:, j * 512 + t128 * 128: j * 512 + t128 * 128 + 128]
                                sl = aT_lo[:, j * 512 + t128 * 128: j * 512 + t128 * 128 + 128]
                                mh = slot_hi(2, par)[:, j * 1024 + hf * 512: j * 1024 + (hf + 1) * 512]
                                ml = slot_lo(2, par)[:, j * 1024 + hf * 512: j * 1024 + (hf + 1) * 512]
                                ev_y = mm3s(pb[bidx][:, :], sh, sl, mh, ml,
                                            j == 0, j == 3,
                                            [a_evs[j], slot_rdy[2][par]]
                                            + ([bfree[bidx]] if j == 0 else []),
                                            want=(j == 3))
                                if j == 3:
                                    y_stop[(t128, hf)] = ev_y

                    def y_evac(t128s, tg=tg, y_stop=y_stop, YB=YB):
                        for t128 in t128s:
                            for hf in range(2):
                                bidx = YB[t128] + hf
                                stl = tg * 4 + t128
                                ev_acc = Sd.ve(lambda e, stl=stl, hf=hf, bidx=bidx: e.scalar_tensor_tensor(
                                    out=acc[:, stl * 1024 + hf * 512: stl * 1024 + (hf + 1) * 512],
                                    in0=pb[bidx][:, :], scalar=1.0,
                                    in1=acc[:, stl * 1024 + hf * 512: stl * 1024 + (hf + 1) * 512],
                                    op0=ALU.mult, op1=ALU.add),
                                    waits=[y_stop[(t128, hf)], ev_acc0])
                                bfree[bidx] = ev_acc

                    for ic in range(4):
                        ev_h1 = None
                        ev_h3 = None
                        for c in range(8):
                            mh = hTm_hi[:, c * 1024 + tg * 512: c * 1024 + tg * 512 + 512]
                            ml = hTm_lo[:, c * 1024 + tg * 512: c * 1024 + tg * 512 + 512]
                            s1h = slot_hi(0, par)[:, c * 512 + ic * 128: c * 512 + ic * 128 + 128]
                            s1l = slot_lo(0, par)[:, c * 512 + ic * 128: c * 512 + ic * 128 + 128]
                            ev_h1 = mm3s(pb[0][:, :], s1h, s1l, mh, ml, c == 0, c == 7,
                                         [slot_rdy[0][par], ev_ht, bfree[0]], want=(c == 7))
                        for c in range(8):
                            mh = hTm_hi[:, c * 1024 + tg * 512: c * 1024 + tg * 512 + 512]
                            ml = hTm_lo[:, c * 1024 + tg * 512: c * 1024 + tg * 512 + 512]
                            s3h = slot_hi(1, par)[:, c * 512 + ic * 128: c * 512 + ic * 128 + 128]
                            s3l = slot_lo(1, par)[:, c * 512 + ic * 128: c * 512 + ic * 128 + 128]
                            ev_h3 = mm3s(pb[1][:, :], s3h, s3l, mh, ml, c == 0, c == 7,
                                         [slot_rdy[1][par], ev_ht, bfree[1]], want=(c == 7))
                        h1_last = ev_h1
                        h3_last = ev_h3
                        ev_sh = Sd.se(lambda e: e.activation(
                            out=sh1b, in_=pb[0][:, :], func=AF.Silu),
                            waits=[ev_h1, sh1_guard, out_gate])
                        bfree[0] = ev_sh
                        ev_h3c = Sd.ve(lambda e, par=par, tg=tg: e.tensor_tensor(
                            out=h3cb, in0=pb[1][:, :],
                            in1=cbuf[par][:, tg * 512:(tg + 1) * 512], op=ALU.mult),
                            waits=[ev_h3, cb_ready[par], h3c_guard])
                        bfree[1] = ev_h3c
                        ev_a = Sd.ve(lambda e: e.tensor_tensor(
                            out=aTf, in0=sh1b, in1=h3cb,
                            op=ALU.mult), waits=[ev_sh, ev_h3c, aT_cons, aTf_guard])
                        ev_ahi = Sd.se(lambda e, ic=ic: e.copy(
                            out=aT_hi[:, ic * 512:(ic + 1) * 512],
                            in_=aTf),
                            waits=[ev_a, aT_cons])
                        ev_alo = Sd.ve(lambda e, ic=ic: e.tensor_tensor(
                            out=aT_lo[:, ic * 512:(ic + 1) * 512],
                            in0=aTf,
                            in1=aT_hi[:, ic * 512:(ic + 1) * 512], op=ALU.subtract),
                            waits=[ev_ahi, aT_cons])
                        sh1_guard = ev_a
                        h3c_guard = ev_a
                        aTf_guard = ev_alo
                        a_evs.append(ev_alo)
                        if ic >= 1:
                            y_partial(ic - 1, (0, 1, 2))
                    y_partial(3, (0, 1, 2))
                    y_evac((0,))
                    for j in range(4):
                        y_partial(j, (3,))
                    y_evac((1, 2, 3))
                    aT_cons = y_stop[(3, 1)]
                slot_free[0][par] = h1_last
                slot_free[1][par] = h3_last
                slot_free[2][par] = aT_cons

            sh_rdy = []
            for k in range(3):
                hbm = (ws1_h, ws3_h, ws2_h)[k]
                ev_lo = None
                for half in range(4):
                    ev_l = Sd.dma("sy", lambda e, hbm=hbm, half=half: e.dma_start(
                        out=wstage.rearrange("p (c i) -> p c i", c=2),
                        in_=hbm[half * 256:(half + 1) * 256].rearrange("(c p) i -> p c i", p=128)),
                        waits=stage_free_m, inc="WSH")
                    hi_ap = shared_hi[k][:, half * 2048:(half + 1) * 2048]
                    lo_ap = shared_lo[k][:, half * 2048:(half + 1) * 2048]
                    ev_hi = Sd.dma("gp", lambda e, hi_ap=hi_ap: e.dma_start(
                        out=hi_ap, in_=wstage),
                        waits=[ev_l, slot_free[k][0]], inc="GPD")
                    ev_lo = Sd.add("gp", lambda e, hi_ap=hi_ap, lo_ap=lo_ap: e.tensor_tensor(
                        out=lo_ap, in0=wstage, in1=hi_ap, op=ALU.subtract),
                        waits=[ev_hi, slot_free[k][1]], inc="GP")
                    stage_free_m = [ev_lo]
                sh_rdy.append(ev_lo)

            for tg in range(2):
                a_evs = []
                for ic in range(8):
                    b1i = 0 if ic % 2 == 0 else 2
                    b3i = 1 if ic % 2 == 0 else 3
                    ev_h1 = None
                    ev_h3 = None
                    for c in range(8):
                        mh = hTm_hi[:, c * 1024 + tg * 512: c * 1024 + tg * 512 + 512]
                        ml = hTm_lo[:, c * 1024 + tg * 512: c * 1024 + tg * 512 + 512]
                        s1h = shared_hi[0][:, c * 1024 + ic * 128: c * 1024 + ic * 128 + 128]
                        s1l = shared_lo[0][:, c * 1024 + ic * 128: c * 1024 + ic * 128 + 128]
                        ev_h1 = mm3s(pb[b1i][:, :], s1h, s1l, mh, ml, c == 0, c == 7,
                                     [sh_rdy[0], ev_ht, bfree[b1i]], want=(c == 7))
                    for c in range(8):
                        mh = hTm_hi[:, c * 1024 + tg * 512: c * 1024 + tg * 512 + 512]
                        ml = hTm_lo[:, c * 1024 + tg * 512: c * 1024 + tg * 512 + 512]
                        s3h = shared_hi[1][:, c * 1024 + ic * 128: c * 1024 + ic * 128 + 128]
                        s3l = shared_lo[1][:, c * 1024 + ic * 128: c * 1024 + ic * 128 + 128]
                        ev_h3 = mm3s(pb[b3i][:, :], s3h, s3l, mh, ml, c == 0, c == 7,
                                     [sh_rdy[1], ev_ht, bfree[b3i]], want=(c == 7))
                    sh_h_last = ev_h3
                    if ic == 4:
                        icg = 0
                        for t128 in range(4):
                            yb0 = 4 + ((t128 + icg) % 2) * 2
                            for hf in range(2):
                                bidx = yb0 + hf
                                ev_y = None
                                for ic2 in range(4):
                                    sh = aT_hi[:, ic2 * 512 + t128 * 128: ic2 * 512 + t128 * 128 + 128]
                                    sl = aT_lo[:, ic2 * 512 + t128 * 128: ic2 * 512 + t128 * 128 + 128]
                                    icc = icg * 4 + ic2
                                    mh = shared_hi[2][:, icc * 1024 + hf * 512: icc * 1024 + (hf + 1) * 512]
                                    ml = shared_lo[2][:, icc * 1024 + hf * 512: icc * 1024 + (hf + 1) * 512]
                                    ev_y = mm3s(pb[bidx][:, :], sh, sl, mh, ml,
                                                ic2 == 0, ic2 == 3,
                                                a_evs[icg * 4:] + [sh_rdy[2], bfree[bidx]],
                                                want=(ic2 == 3))
                                stl = tg * 4 + t128
                                ev_acc = Sd.ve(lambda e, stl=stl, hf=hf, bidx=bidx: e.scalar_tensor_tensor(
                                    out=acc[:, stl * 1024 + hf * 512: stl * 1024 + (hf + 1) * 512],
                                    in0=pb[bidx][:, :], scalar=1.0,
                                    in1=acc[:, stl * 1024 + hf * 512: stl * 1024 + (hf + 1) * 512],
                                    op0=ALU.mult, op1=ALU.add),
                                    waits=[ev_y, ev_acc0])
                                bfree[bidx] = ev_acc
                        aT_cons = ev_y
                    ev_sh = Sd.se(lambda e, b1i=b1i: e.activation(
                        out=sh1b, in_=pb[b1i][:, :], func=AF.Silu),
                        waits=[ev_h1, sh1_guard])
                    bfree[b1i] = ev_sh
                    ev_a = Sd.ve(lambda e, b3i=b3i: e.tensor_tensor(
                        out=aTf, in0=sh1b, in1=pb[b3i][:, :],
                        op=ALU.mult), waits=[ev_sh, ev_h3, aT_cons, h3c_guard, aTf_guard])
                    bfree[b3i] = ev_a
                    ev_ahi = Sd.se(lambda e, ic=ic: e.copy(
                        out=aT_hi[:, (ic % 4) * 512:((ic % 4) + 1) * 512],
                        in_=aTf),
                        waits=[ev_a, aT_cons])
                    ev_alo = Sd.ve(lambda e, ic=ic: e.tensor_tensor(
                        out=aT_lo[:, (ic % 4) * 512:((ic % 4) + 1) * 512],
                        in0=aTf,
                        in1=aT_hi[:, (ic % 4) * 512:((ic % 4) + 1) * 512], op=ALU.subtract),
                        waits=[ev_ahi, aT_cons])
                    sh1_guard = ev_a
                    h3c_guard = ev_a
                    aTf_guard = ev_alo
                    a_evs.append(ev_alo)
                    if ic == 7:
                        icg = 1
                        for t128 in range(4):
                            yb0 = 4 + ((t128 + icg) % 2) * 2
                            for hf in range(2):
                                bidx = yb0 + hf
                                ev_y = None
                                for ic2 in range(4):
                                    sh = aT_hi[:, ic2 * 512 + t128 * 128: ic2 * 512 + t128 * 128 + 128]
                                    sl = aT_lo[:, ic2 * 512 + t128 * 128: ic2 * 512 + t128 * 128 + 128]
                                    icc = icg * 4 + ic2
                                    mh = shared_hi[2][:, icc * 1024 + hf * 512: icc * 1024 + (hf + 1) * 512]
                                    ml = shared_lo[2][:, icc * 1024 + hf * 512: icc * 1024 + (hf + 1) * 512]
                                    ev_y = mm3s(pb[bidx][:, :], sh, sl, mh, ml,
                                                ic2 == 0, ic2 == 3,
                                                a_evs[icg * 4:] + [sh_rdy[2], bfree[bidx]],
                                                want=(ic2 == 3))
                                stl = tg * 4 + t128
                                ev_acc = Sd.ve(lambda e, stl=stl, hf=hf, bidx=bidx: e.scalar_tensor_tensor(
                                    out=acc[:, stl * 1024 + hf * 512: stl * 1024 + (hf + 1) * 512],
                                    in0=pb[bidx][:, :], scalar=1.0,
                                    in1=acc[:, stl * 1024 + hf * 512: stl * 1024 + (hf + 1) * 512],
                                    op0=ALU.mult, op1=ALU.add),
                                    waits=[ev_y, ev_acc0])
                                bfree[bidx] = ev_acc
                        aT_cons = ev_y

            ln_pe_wait = ("PE", Sd.cnt["PE"])
            for st in range(8):
                ev_hl = Sd.dma("se", lambda e, st=st, hh=hh: e.dma_start(
                    out=hln, in_=h_scr[hh * 1024 + st * 128: hh * 1024 + (st + 1) * 128, :]),
                    waits=[hln_free, hw_total, ln_pe_wait], inc="HR")
                ev_r2 = Sd.ve(lambda e, st=st: e.tensor_tensor(
                    out=res2t, in0=acc[:, st * 1024:(st + 1) * 1024], in1=hln, op=ALU.add),
                    waits=[ev_hl, bfree[4], bfree[5], bfree[6], bfree[7], aT_cons,
                           res2_guard])
                hln_free = ev_r2
                ev_s1 = Sd.ve(lambda e: e.reduce_sum(out=ssum2, in_=res2t, axis=AX.X),
                              waits=[ev_r2])
                ev_nm = Sd.ve(lambda e: e.tensor_scalar_mul(out=negm2, in0=ssum2,
                                                            scalar1=-1.0 / D), waits=[ev_s1])
                ev_sq = Sd.se(lambda e: e.activation(out=sqd2, in_=res2t, func=AF.Square,
                                                     accum_out=ss22),
                              waits=[ev_r2])
                ev_m2 = Sd.ve(lambda e: e.tensor_tensor(out=m22, in0=negm2, in1=negm2,
                                                        op=ALU.mult), waits=[ev_nm])
                ev_var = Sd.ve(lambda e: e.tensor_scalar(
                    out=var2, in0=ss22, scalar1=1.0 / D, scalar2=m22, op0=ALU.mult,
                    op1=ALU.subtract), waits=[ev_sq, ev_m2])
                ev_sv = Sd.se(lambda e: e.activation(out=sv2, in_=var2, func=AF.Sqrt,
                                                     bias=epsc), waits=[ev_var, ev_eps])
                ev_rs = Sd.ve(lambda e: e.reciprocal(out=rstd2, in_=sv2), waits=[ev_sv])
                ev_o = Sd.ve(lambda e: e.tensor_scalar(
                    out=outt, in0=res2t, scalar1=negm2, scalar2=rstd2, op0=ALU.add,
                    op1=ALU.mult), waits=[ev_rs, out_guard])
                res2_guard = ev_o
                out_guard = Sd.dma("gp", lambda e, st=st, hh=hh: e.dma_start(
                    out=out_h[hh * 1024 + st * 128: hh * 1024 + (st + 1) * 128, :],
                    in_=outt), waits=[ev_o], inc="OUT")

            half_end_bar = [("PE", Sd.cnt["PE"]), ("VE", Sd.cnt["VE"]),
                            ("SE", Sd.cnt["SE"]), ("OUT", Sd.cnt["OUT"])]

        out_total = Sd.cnt["OUT"]
        Sd.add("gp", lambda e: e.wait_ge(sems["OUT"], out_total))

        with nc.Block() as block:
            def mk(engname):
                def run(eng):
                    for fn, ws, inc, n in Sd.ops[engname]:
                        for s_, v_ in ws:
                            eng.wait_ge(sems[s_], v_)
                        ins = fn(eng)
                        if inc is not None:
                            ins.then_inc(sems[inc], n)
                return run
            block.tensor(mk("pe"))
            block.vector(mk("ve"))
            block.scalar(mk("se"))
            block.sync(mk("sy"))
            block.gpsimd(mk("gp"))
    return nc


_NC_CACHE = [None]


def _aux_array():
    a = np.zeros((128, 2304), dtype=np.float32)
    a[:, 0:128] = np.eye(128, dtype=np.float32)
    for e in range(NE):
        a[e, 128 + e * 128:128 + (e + 1) * 128] = 1.0
    a[0, 2176:2304] = 1.0
    return a


def kernel(L0_attn_x, L0_attn_Wq, L0_attn_bq, L0_attn_Wk, L0_attn_bk,
           L0_attn_Wv, L0_attn_bv, L0_attn_Wg, L0_attn_bg, L0_attn_Wo,
           L0_attn_bo, L0_ffn_gate_w, L0_ffn_w1, L0_ffn_w2, L0_ffn_w3,
           L0_ffn_ws1, L0_ffn_ws2, L0_ffn_ws3):
    x = np.ascontiguousarray(np.asarray(L0_attn_x, dtype=np.float32))
    c32 = lambda a: np.ascontiguousarray(np.asarray(a, dtype=np.float32))
    base = dict(Wq=c32(L0_attn_Wq), bq=c32(L0_attn_bq), Wk=c32(L0_attn_Wk),
                bk=c32(L0_attn_bk), Wv=c32(L0_attn_Wv), bv=c32(L0_attn_bv),
                Wg=c32(L0_attn_Wg), bg=c32(L0_attn_bg), Wo=c32(L0_attn_Wo),
                bo=c32(L0_attn_bo), gate_w=c32(L0_ffn_gate_w), w1=c32(L0_ffn_w1),
                w2=c32(L0_ffn_w2), w3=c32(L0_ffn_w3), ws1=c32(L0_ffn_ws1),
                ws2=c32(L0_ffn_ws2), ws3=c32(L0_ffn_ws3), aux=_aux_array())
    if _NC_CACHE[0] is None:
        _NC_CACHE[0] = build_program()
    nc = _NC_CACHE[0]
    in_maps = [dict(base, x=x[c]) for c in range(NCORE)]
    res = run_bass_kernel_spmd(nc, in_maps, core_ids=list(range(NCORE)))
    return np.stack([np.asarray(res.results[c]["out"]) for c in range(NCORE)], axis=0)


# version 16
# speedup vs baseline: 1.0958x; 1.0958x over previous
import numpy as np
from contextlib import ExitStack

import concourse.bass as bass
import concourse.mybir as mybir
import concourse.bass_utils as _bu
from concourse.bass_utils import run_bass_kernel_spmd

if not getattr(_bu, "_ldw_opt_patched", False):
    _orig_run_command = _bu.run_command

    def _run_command_ldw(cmd, **kw):
        cmd = [c.replace("--enable-ldw-opt=false", "--enable-ldw-opt=true")
               if isinstance(c, str) else c for c in cmd]
        return _orig_run_command(cmd, **kw)

    _bu.run_command = _run_command_ldw
    _bu._ldw_opt_patched = True

F32 = mybir.dt.float32
BF16 = mybir.dt.bfloat16
AF = mybir.ActivationFunctionType
ALU = mybir.AluOpType
AX = mybir.AxisListType

S = 2048
D = 1024
NT = S // 128
NCH = 8
NE = 16
EI = 512
SI = 1024
LN_EPS = 1e-5
NCORE = 8

ENGINES = ("pe", "ve", "se", "sy", "gp")


def _flat(waits):
    out = []
    for w in waits:
        if w is None:
            continue
        if isinstance(w, list):
            out.extend(x for x in w if x is not None)
        else:
            out.append(w)
    return out


class Sched:
    def __init__(self):
        self.ops = {e: [] for e in ENGINES}
        self.cnt = {}
        self.maxw = {e: {} for e in ENGINES}

    def sem(self, name):
        self.cnt[name] = 0

    def add(self, eng, fn, waits=(), inc=None, n=1):
        ws = []
        for w in _flat(waits):
            s, v = w
            if v <= 0 or v <= self.maxw[eng].get(s, 0):
                continue
            self.maxw[eng][s] = v
            ws.append((s, v))
        ev = None
        if inc is not None:
            self.cnt[inc] += n
            ev = (inc, self.cnt[inc])
        self.ops[eng].append((fn, tuple(ws), inc, n))
        return ev

    def pe(self, fn, waits=(), want=True):
        return self.add("pe", fn, waits, "PE" if want else None)

    def ve(self, fn, waits=()):
        return self.add("ve", fn, waits, "VE")

    def se(self, fn, waits=()):
        return self.add("se", fn, waits, "SE")

    def dma(self, eng, fn, waits=(), inc=None):
        return self.add(eng, fn, waits, inc, n=16)


def build_program():
    nc = bass.Bass("TRN2", target_bir_lowering=False)

    xin = nc.dram_tensor("x", [S, D], F32, kind="ExternalInput").ap()
    wq_h = nc.dram_tensor("Wq", [D, D], F32, kind="ExternalInput").ap()
    bq_h = nc.dram_tensor("bq", [D], F32, kind="ExternalInput").ap()
    wk_h = nc.dram_tensor("Wk", [D, D], F32, kind="ExternalInput").ap()
    bk_h = nc.dram_tensor("bk", [D], F32, kind="ExternalInput").ap()
    wv_h = nc.dram_tensor("Wv", [D, D], F32, kind="ExternalInput").ap()
    bv_h = nc.dram_tensor("bv", [D], F32, kind="ExternalInput").ap()
    wg_h = nc.dram_tensor("Wg", [D, D], F32, kind="ExternalInput").ap()
    bg_h = nc.dram_tensor("bg", [D], F32, kind="ExternalInput").ap()
    wo_h = nc.dram_tensor("Wo", [D, D], F32, kind="ExternalInput").ap()
    bo_h = nc.dram_tensor("bo", [D], F32, kind="ExternalInput").ap()
    gw_h = nc.dram_tensor("gate_w", [NE, D], F32, kind="ExternalInput").ap()
    w1_h = nc.dram_tensor("w1", [NE, D, EI], F32, kind="ExternalInput").ap()
    w2_h = nc.dram_tensor("w2", [NE, EI, D], F32, kind="ExternalInput").ap()
    w3_h = nc.dram_tensor("w3", [NE, D, EI], F32, kind="ExternalInput").ap()
    ws1_h = nc.dram_tensor("ws1", [D, SI], F32, kind="ExternalInput").ap()
    ws2_h = nc.dram_tensor("ws2", [SI, D], F32, kind="ExternalInput").ap()
    ws3_h = nc.dram_tensor("ws3", [D, SI], F32, kind="ExternalInput").ap()
    aux_h = nc.dram_tensor("aux", [128, 2304], F32, kind="ExternalInput").ap()
    out_h = nc.dram_tensor("out", [S, D], F32, kind="ExternalOutput").ap()
    h_scr = nc.dram_tensor("h_scr", [S, D], F32, kind="Internal").ap()
    hth_scr = nc.dram_tensor("hth_scr", [8, 128, S], BF16, kind="Internal").ap()
    htl_scr = nc.dram_tensor("htl_scr", [8, 128, S], BF16, kind="Internal").ap()

    Sd = Sched()
    for nm in ("PE", "VE", "SE", "GP", "GPD", "LDI", "LDB", "XA0", "XA1", "WA",
               "XB0", "XB1", "XBP0", "XBP1", "WB", "WB2", "HW", "HTW", "HT",
               "HR", "OUT", "SWL", "WSH"):
        Sd.sem(nm)

    bfree = {i: None for i in range(8)}

    with ExitStack() as ctx:
        PW = ctx.enter_context(nc.sbuf_tensor("PW", [128, 26624], F32))
        PA = ctx.enter_context(nc.sbuf_tensor("PA", [128, 16384], F32))
        PM = ctx.enter_context(nc.sbuf_tensor("PM", [128, 5952], F32))
        PS = ctx.enter_context(nc.sbuf_tensor("PS", [128, 4096], F32))
        pb = [ctx.enter_context(nc.psum_tensor("pb%d" % i, [128, 512], F32))
              for i in range(8)]
        sems = {nm: ctx.enter_context(nc.semaphore(nm)) for nm in Sd.cnt}

        ident = PM[:, 0:128]
        bq_col = PM[:, 128:136]
        bg_col = PM[:, 136:144]
        bkb = PM[:, 144:1168]
        bvb = PM[:, 1168:2192]
        bob = PM[:, 2192:3216]
        onesr = PM[0:1, 3216:3344]
        kvbd = PM[:, 3344:4368]
        kvbd_hi = PM[:, 4368:4880].bitcast(BF16)
        kvbd_lo = PM[:, 4880:5392].bitcast(BF16)
        epsc = PM[:, 5936:5937]
        gwT_hi = PM[:, 5776:5840].bitcast(BF16)
        gwT_lo = PM[:, 5840:5904].bitcast(BF16)
        onehot = PS[:, 0:2048]

        brow = PS[0:1, 2048:3072]
        btmp = PS[0:8, 3328:3456]
        g16 = PS[0:16, 2048:3072]
        gwTf = PS[:, 3456:3584]

        wstage = PW[:, 24576:26624]

        Sd.dma("sy", lambda e: e.dma_start(out=ident, in_=aux_h[:, 0:128]), inc="LDI")
        Sd.dma("sy", lambda e: e.dma_start(out=onehot, in_=aux_h[:, 128:2176]), inc="LDI")
        Sd.dma("sy", lambda e: e.dma_start(out=PM[:, 3216:3344], in_=aux_h[:, 2176:2304]), inc="LDI")
        e_ldi = ("LDI", 48)

        ev_eps = Sd.ve(lambda e: e.memset(epsc, LN_EPS))

        xa_pre = [None, None]
        for t in range(2):
            xa_pre[t] = Sd.dma("sy", lambda e, t=t: e.dma_start(
                out=PA[:, t * 1024:(t + 1) * 1024], in_=xin[t * 128:(t + 1) * 128, :]),
                inc="XA%d" % t)

        def bias_bcast(b_hbm, dest):
            ev_l = Sd.dma("sy", lambda e, b=b_hbm: e.dma_start(out=brow, in_=b.unsqueeze(0)),
                          waits=[("PE", Sd.cnt["PE"])], inc="LDB")
            last = None
            for hf in range(2):
                ev_mm = Sd.pe(lambda e, hf=hf: e.matmul(
                    pb[2][:, :], onesr, brow[:, hf * 512:(hf + 1) * 512],
                    start=True, stop=True), waits=[ev_l, e_ldi, bfree[2]])
                last = Sd.ve(lambda e, hf=hf, d=dest: e.tensor_copy(
                    out=d[:, hf * 512:(hf + 1) * 512], in_=pb[2][:, :]), waits=[ev_mm])
                bfree[2] = last
            return last

        ev_bkb = bias_bcast(bk_h, bkb)
        ev_bvb = bias_bcast(bv_h, bvb)
        ev_bob = bias_bcast(bo_h, bob)

        def bias_col(b_hbm, dest):
            ev_l = Sd.dma("sy", lambda e, b=b_hbm: e.dma_start(
                out=btmp, in_=b.rearrange("(c p) -> c p", p=128)),
                waits=[("PE", Sd.cnt["PE"])], inc="LDB")
            ev_t = Sd.pe(lambda e: e.transpose(pb[3][:, 0:8], btmp, ident[0:8, 0:8]),
                         waits=[ev_l, e_ldi, bfree[3]])
            ev_e = Sd.ve(lambda e, d=dest: e.tensor_copy(out=d, in_=pb[3][:, 0:8]),
                         waits=[ev_t])
            bfree[3] = ev_e
            return ev_e

        ev_bq = bias_col(bq_h, bq_col)
        ev_bg = bias_col(bg_h, bg_col)

        ev_gwl = Sd.dma("sy", lambda e: e.dma_start(out=g16, in_=gw_h),
                        waits=[("PE", Sd.cnt["PE"])], inc="LDB")
        ev_gwf = None
        for c in range(8):
            ev_t = Sd.pe(lambda e, c=c: e.transpose(
                pb[3][:, 128:144], g16[0:16, c * 128:(c + 1) * 128], ident[0:16, 0:16]),
                waits=[ev_gwl, e_ldi, bfree[3]])
            ev_gwf = Sd.ve(lambda e, c=c: e.tensor_copy(
                out=gwTf[:, c * 16:(c + 1) * 16], in_=pb[3][:, 128:144]), waits=[ev_t])
            bfree[3] = ev_gwf
        ev_gwh = Sd.se(lambda e: e.copy(out=gwT_hi, in_=gwTf), waits=[ev_gwf])
        ev_gwT = Sd.ve(lambda e: e.tensor_tensor(out=gwT_lo, in0=gwTf, in1=gwT_hi,
                                                 op=ALU.subtract), waits=[ev_gwh])

        def wsplit(stage_ap, hi_ap, lo_ap, waits_se=(), waits_ve=()):
            ev_hi = Sd.se(lambda e: e.copy(out=hi_ap, in_=stage_ap), waits=list(waits_se))
            ev_lo = Sd.ve(lambda e: e.tensor_tensor(out=lo_ap, in0=stage_ap, in1=hi_ap,
                                                    op=ALU.subtract),
                          waits=[ev_hi] + list(waits_ve))
            return ev_hi, ev_lo

        wk_hi = PW[:, 0:4096].bitcast(BF16)
        wk_lo = PW[:, 4096:8192].bitcast(BF16)
        wv_hi = PW[:, 8192:12288].bitcast(BF16)
        wv_lo = PW[:, 12288:16384].bitcast(BF16)
        wa_ev = [[None] * 4, [None] * 4]
        stage_free = None
        for half in range(4):
            for wi, hbm in ((0, wk_h), (1, wv_h)):
                ev_l = Sd.dma("sy", lambda e, hbm=hbm, half=half: e.dma_start(
                    out=wstage.rearrange("p (c i) -> p c i", c=2),
                    in_=hbm[half * 256:(half + 1) * 256].rearrange("(c p) i -> p c i", p=128)),
                    waits=[stage_free], inc="WA")
                hi = (wk_hi, wv_hi)[wi]
                lo = (wk_lo, wv_lo)[wi]
                ev_hi, ev_lo = wsplit(
                    wstage, hi[:, half * 2048:(half + 1) * 2048],
                    lo[:, half * 2048:(half + 1) * 2048], waits_se=[ev_l])
                stage_free = ev_lo
                wa_ev[wi][half] = ev_lo

        wq_hi = PW[:, 16384:20480].bitcast(BF16)
        wq_lo = PW[:, 20480:24576].bitcast(BF16)
        wg_hi = PW[:, 8192:12288].bitcast(BF16)
        wg_lo = PW[:, 12288:16384].bitcast(BF16)
        wo_hi = PW[:, 0:4096].bitcast(BF16)
        wo_lo = PW[:, 4096:8192].bitcast(BF16)
        wb_ev = [[None] * 4, [None] * 4, [None] * 4]
        wq_chain = stage_free

        def mm3s(bk_ap, sh, sl, mh, ml, first, last, waits0, want=True):
            Sd.pe(lambda e: e.matmul(bk_ap, sh, mh, start=first, stop=False),
                  waits=waits0, want=False)
            Sd.pe(lambda e: e.matmul(bk_ap, sh, ml, start=False, stop=False), want=False)
            return Sd.pe(lambda e: e.matmul(bk_ap, sl, mh, start=False, stop=last),
                         want=want)

        xa = [PA[:, 0:1024], PA[:, 1024:2048]]
        xT_hi = [PA[:, 2048:2560].bitcast(BF16), PA[:, 3072:3584].bitcast(BF16)]
        xT_lo = [PA[:, 2560:3072].bitcast(BF16), PA[:, 3584:4096].bitcast(BF16)]
        kkb = PA[:, 4096:5120]
        kk_hi = PA[:, 5120:5632].bitcast(BF16)
        kk_lo = PA[:, 5632:6144].bitcast(BF16)
        vvb = PA[:, 6144:7168]
        vv_hi = PA[:, 7168:7680].bitcast(BF16)
        vv_lo = PA[:, 7680:8192].bitcast(BF16)
        t1b = PA[:, 8192:9216]
        t2b = PA[:, 9216:10240]

        xa_free = [None, None]
        xa_ready = [None, None]
        xT_free = [None, None]
        kk_guard = None
        kv_last = None

        for t in range(NT):
            p = t % 2
            if t < 2:
                xa_ready[p] = xa_pre[t]
            else:
                xa_ready[p] = Sd.dma("sy", lambda e, t=t, p=p: e.dma_start(
                    out=xa[p], in_=xin[t * 128:(t + 1) * 128, :]),
                    waits=[xa_free[p]], inc="XA%d" % p)
            ev_xT = [None, None]
            for bi in range(2):
                ev_tr = None
                for j in range(4):
                    c = bi * 4 + j
                    ev_tr = Sd.pe(lambda e, c=c, p=p, bi=bi, j=j: e.transpose(
                        pb[bi][:, j * 128:(j + 1) * 128],
                        xa[p][:, c * 128:(c + 1) * 128], ident),
                        waits=[xa_ready[p], e_ldi, bfree[bi]], want=(j == 3))
                ev_hi = Sd.se(lambda e, p=p, bi=bi: e.copy(
                    out=xT_hi[p][:, bi * 512:(bi + 1) * 512], in_=pb[bi][:, :]),
                    waits=[ev_tr, xT_free[p]])
                ev_lo = Sd.ve(lambda e, p=p, bi=bi: e.tensor_tensor(
                    out=xT_lo[p][:, bi * 512:(bi + 1) * 512], in0=pb[bi][:, :],
                    in1=xT_hi[p][:, bi * 512:(bi + 1) * 512], op=ALU.subtract),
                    waits=[ev_hi, xT_free[p]])
                bfree[bi] = ev_lo
                ev_xT[bi] = ev_lo
            xa_free[p] = ev_tr

            def proj(whi, wlo, bank0, wi_, p=p, ev_xT=ev_xT):
                stops = []
                for hf in range(2):
                    ev = None
                    for c in range(8):
                        sh = xT_hi[p][:, c * 128:(c + 1) * 128]
                        sl = xT_lo[p][:, c * 128:(c + 1) * 128]
                        mh = whi[:, c * 1024 + hf * 512: c * 1024 + (hf + 1) * 512]
                        ml = wlo[:, c * 1024 + hf * 512: c * 1024 + (hf + 1) * 512]
                        w0 = [ev_xT[0], ev_xT[1], bfree[bank0 + hf], wa_ev[wi_][c // 2]]
                        ev = mm3s(pb[bank0 + hf][:, :], sh, sl, mh, ml,
                                  c == 0, c == 7, w0, want=(c == 7))
                    stops.append(ev)
                return stops

            k_stops = proj(wk_hi, wk_lo, 2, 0)
            v_stops = proj(wv_hi, wv_lo, 4, 1)
            xT_free[p] = v_stops[-1]

            ev_kpre = []
            for hf in range(2):
                ev_kpre.append(Sd.ve(lambda e, hf=hf: e.tensor_tensor(
                    out=kkb[:, hf * 512:(hf + 1) * 512], in0=pb[2 + hf][:, :],
                    in1=bkb[:, hf * 512:(hf + 1) * 512], op=ALU.add),
                    waits=[k_stops[hf], ev_bkb, kk_guard]))
                bfree[2 + hf] = ev_kpre[hf]
            ev_min = Sd.ve(lambda e: e.tensor_scalar(
                out=t1b, in0=kkb, scalar1=0.0, scalar2=None, op0=ALU.min),
                waits=list(ev_kpre))
            ev_exp = Sd.se(lambda e: e.activation(out=t2b, in_=t1b, func=AF.Exp),
                           waits=[ev_min])
            kk_ready = Sd.ve(lambda e: e.scalar_tensor_tensor(
                out=kkb, in0=kkb, scalar=0.0, in1=t2b, op0=ALU.max, op1=ALU.add),
                waits=[ev_exp])
            ev_kh = Sd.se(lambda e: e.copy(out=kk_hi, in_=kkb), waits=[kk_ready, kk_guard])
            ev_kl = Sd.ve(lambda e: e.tensor_tensor(out=kk_lo, in0=kkb, in1=kk_hi,
                                                    op=ALU.subtract), waits=[ev_kh])
            ev_vpre = []
            for hf in range(2):
                ev_vpre.append(Sd.ve(lambda e, hf=hf: e.tensor_tensor(
                    out=vvb[:, hf * 512:(hf + 1) * 512], in0=pb[4 + hf][:, :],
                    in1=bvb[:, hf * 512:(hf + 1) * 512], op=ALU.add),
                    waits=[v_stops[hf], ev_bvb, kk_guard]))
                bfree[4 + hf] = ev_vpre[hf]
            ev_vh = Sd.se(lambda e: e.copy(out=vv_hi, in_=vvb),
                          waits=[ev_vpre[0], ev_vpre[1], kk_guard])
            ev_vl = Sd.ve(lambda e: e.tensor_tensor(out=vv_lo, in0=vvb, in1=vv_hi,
                                                    op=ALU.subtract), waits=[ev_vh])

            for pr in range(8):
                co = (pr % 4) * 128
                bk_i = 6 + pr // 4
                sh = kk_hi[:, pr * 128:(pr + 1) * 128]
                sl = kk_lo[:, pr * 128:(pr + 1) * 128]
                mh = vv_hi[:, pr * 128:(pr + 1) * 128]
                ml = vv_lo[:, pr * 128:(pr + 1) * 128]
                last3 = (t == NT - 1 and pr % 4 == 3)
                first3 = (t == 0 and pr % 4 == 0)
                Sd.pe(lambda e, sh=sh, mh=mh, bk_i=bk_i, co=co, f3=first3: e.matmul(
                    pb[bk_i][:, co:co + 128], sh, mh, start=f3, stop=False),
                    waits=[ev_kl, ev_vl], want=False)
                Sd.pe(lambda e, sh=sh, ml=ml, bk_i=bk_i, co=co: e.matmul(
                    pb[bk_i][:, co:co + 128], sh, ml, start=False, stop=False), want=False)
                kv_last = Sd.pe(lambda e, sl=sl, mh=mh, bk_i=bk_i, co=co, l3=last3: e.matmul(
                    pb[bk_i][:, co:co + 128], sl, mh, start=False, stop=l3),
                    want=(pr == 7))
            kk_guard = kv_last

            if t in (6, 8, 10, 12):
                half = (t - 6) // 2
                ev_l = Sd.dma("se", lambda e, half=half: e.dma_start(
                    out=wstage.rearrange("p (c i) -> p c i", c=2),
                    in_=wq_h[half * 256:(half + 1) * 256].rearrange("(c p) i -> p c i", p=128)),
                    waits=[wq_chain], inc="WB")
                hi_ap = wq_hi[:, half * 2048:(half + 1) * 2048]
                lo_ap = wq_lo[:, half * 2048:(half + 1) * 2048]
                ev_hi = Sd.se(lambda e, hi_ap=hi_ap: e.copy(out=hi_ap, in_=wstage),
                              waits=[ev_l])
                ev_lo = Sd.add("gp", lambda e, hi_ap=hi_ap, lo_ap=lo_ap: e.tensor_tensor(
                    out=lo_ap, in0=wstage, in1=hi_ap, op=ALU.subtract),
                    waits=[ev_hi], inc="GP")
                wq_chain = ev_lo
                wb_ev[0][half] = ev_lo

        a_ve_tail = ("VE", Sd.cnt["VE"])
        ev_z = Sd.ve(lambda e: e.memset(kvbd, 0.0))
        kv_f32 = None
        for pr in range(8):
            for hv in range(2):
                lo = hv * 64
                kv_f32 = Sd.ve(lambda e, pr=pr, lo=lo: e.tensor_copy(
                    out=kvbd[lo:lo + 64, pr * 128 + lo: pr * 128 + lo + 64],
                    in_=pb[6 + pr // 4][lo:lo + 64, (pr % 4) * 128 + lo:(pr % 4) * 128 + lo + 64]),
                    waits=[kv_last, ev_z])
        ev_kvh = Sd.se(lambda e: e.copy(out=kvbd_hi, in_=kvbd), waits=[kv_f32])
        kv_ev = Sd.ve(lambda e: e.tensor_tensor(out=kvbd_lo, in0=kvbd, in1=kvbd_hi,
                                                op=ALU.subtract), waits=[ev_kvh])
        bfree[6] = kv_f32
        bfree[7] = kv_f32

        barA_pe = ("PE", Sd.cnt["PE"])
        barA = [barA_pe, ("VE", Sd.cnt["VE"]), ("SE", Sd.cnt["SE"])]

        xb = [PA[:, 0:2048], PA[:, 2048:4096]]
        xTb_hi = PA[:, 4096:5120].bitcast(BF16)
        xTb_lo = PA[:, 5120:6144].bitcast(BF16)
        qtf = PA[:, 6144:6656]
        qT_hi = PA[:, 6656:7680].bitcast(BF16)
        qT_lo = PA[:, 7680:8704].bitcast(BF16)
        gsb = PA[:, 8704:10752]
        gaf = PA[:, 10752:11264]
        ga_hi = PA[:, 11264:12288].bitcast(BF16)
        ga_lo = PA[:, 12288:13312].bitcast(BF16)
        res1 = PA[:, 13312:14336]
        t1c = PA[:, 14336:14848]
        t2c = PA[:, 14848:15360]
        sqd = PA[:, 15360:16384]
        hTb_hi = PS[:, 2048:3072].bitcast(BF16)
        hTb_lo = PS[:, 3072:4096].bitcast(BF16)
        stat = PM[:, 5904:5920]
        ssum = stat[:, 0:1]
        negm = stat[:, 1:2]
        ss2 = stat[:, 2:3]
        m2v = stat[:, 3:4]
        varv = stat[:, 4:5]
        svv = stat[:, 5:6]
        rstd = stat[:, 6:7]

        xb_free = [[barA_pe, a_ve_tail], [barA_pe, a_ve_tail]]
        xb_ready = [None, None]
        xb_pre = [None, None]
        for cc0 in range(2):
            ev = None
            for st2 in range(2):
                ev = Sd.dma("gp", lambda e, cc0=cc0, st2=st2: e.dma_start(
                    out=xb[cc0][:, st2 * 1024:(st2 + 1) * 1024],
                    in_=xin[cc0 * 256 + st2 * 128: cc0 * 256 + (st2 + 1) * 128, :]),
                    waits=[barA_pe, a_ve_tail], inc="XBP%d" % cc0)
            xb_pre[cc0] = ev

        stage2 = PM[:, 144:2192]
        wg_chain = wq_chain
        wo_chain = None
        for half in range(4):
            ev_l = Sd.dma("se", lambda e, half=half: e.dma_start(
                out=wstage.rearrange("p (c i) -> p c i", c=2),
                in_=wg_h[half * 256:(half + 1) * 256].rearrange("(c p) i -> p c i", p=128)),
                waits=[wg_chain], inc="WB")
            hi_ap = wg_hi[:, half * 2048:(half + 1) * 2048]
            lo_ap = wg_lo[:, half * 2048:(half + 1) * 2048]
            ev_hi = Sd.dma("gp", lambda e, hi_ap=hi_ap: e.dma_start(
                out=hi_ap, in_=wstage),
                waits=[ev_l, barA_pe] if half == 0 else [ev_l], inc="GPD")
            ev_lo = Sd.add("gp", lambda e, hi_ap=hi_ap, lo_ap=lo_ap: e.tensor_tensor(
                out=lo_ap, in0=wstage, in1=hi_ap, op=ALU.subtract),
                waits=[ev_hi], inc="GP")
            wg_chain = ev_lo
            wb_ev[1][half] = ev_lo

            ev_l2 = Sd.dma("sy", lambda e, half=half: e.dma_start(
                out=stage2.rearrange("p (c i) -> p c i", c=2),
                in_=wo_h[half * 256:(half + 1) * 256].rearrange("(c p) i -> p c i", p=128)),
                waits=(list(barA) if half == 0 else [wo_chain]), inc="WB2")
            hi_ap2 = wo_hi[:, half * 2048:(half + 1) * 2048]
            lo_ap2 = wo_lo[:, half * 2048:(half + 1) * 2048]
            ev_hi2 = Sd.dma("gp", lambda e, hi_ap2=hi_ap2: e.dma_start(
                out=hi_ap2, in_=stage2),
                waits=[ev_l2, barA_pe] if half == 0 else [ev_l2], inc="GPD")
            ev_lo2 = Sd.add("gp", lambda e, hi_ap2=hi_ap2, lo_ap2=lo_ap2: e.tensor_tensor(
                out=lo_ap2, in0=stage2, in1=hi_ap2, op=ALU.subtract),
                waits=[ev_hi2], inc="GP")
            wo_chain = ev_lo2
            wb_ev[2][half] = ev_lo2
        stage_free_b = [wg_chain]
        stage2_guard = wo_chain
        xTb_cons = None
        qT_cons = None
        ga_cons = None
        gs_cons = None
        qtf_free = [None, None]
        gaf_free = [None, None]
        sq_guard = []
        hTb_guard = None

        for cc in range(NCH):
            p = cc % 2
            if cc < 2:
                xb_ready[p] = xb_pre[cc]
            else:
                ev = None
                for st2 in range(2):
                    ev = Sd.dma("sy", lambda e, cc=cc, st2=st2, p=p: e.dma_start(
                        out=xb[p][:, st2 * 1024:(st2 + 1) * 1024],
                        in_=xin[cc * 256 + st2 * 128: cc * 256 + (st2 + 1) * 128, :]),
                        waits=[xb_free[p]], inc="XB%d" % p)
                xb_ready[p] = ev

            ev_xtev = []
            for wv_i in range(4):
                st2, cg = wv_i // 2, wv_i % 2
                bi = wv_i % 2
                ev_tr = None
                for j in range(4):
                    c = cg * 4 + j
                    ev_tr = Sd.pe(lambda e, st2=st2, c=c, p=p, bi=bi, j=j: e.transpose(
                        pb[bi][:, j * 128:(j + 1) * 128],
                        xb[p][:, st2 * 1024 + c * 128: st2 * 1024 + (c + 1) * 128], ident),
                        waits=[xb_ready[p], bfree[bi]], want=(j == 3))
                hi3 = xTb_hi.rearrange("q (c t) -> q c t", c=8)[:, cg * 4:(cg + 1) * 4, st2 * 128:(st2 + 1) * 128]
                lo3 = xTb_lo.rearrange("q (c t) -> q c t", c=8)[:, cg * 4:(cg + 1) * 4, st2 * 128:(st2 + 1) * 128]
                ps3 = pb[bi].rearrange("q (j t) -> q j t", j=4)[:, :, :]
                ev_hi = Sd.se(lambda e, hi3=hi3, ps3=ps3: e.copy(out=hi3, in_=ps3),
                              waits=[ev_tr, xTb_cons])
                ev_lo = Sd.ve(lambda e, lo3=lo3, ps3=ps3, hi3=hi3: e.tensor_tensor(
                    out=lo3, in0=ps3, in1=hi3, op=ALU.subtract),
                    waits=[ev_hi, xTb_cons])
                bfree[bi] = ev_lo
                ev_xtev.append(ev_lo)

            qT_ready = []
            for ec in range(8):
                bk_i = 2 + ec % 2
                par = ec % 2
                ev_q = None
                for c in range(8):
                    ev_q = mm3s(pb[bk_i][:, 0:256],
                                wq_hi[:, c * 1024 + ec * 128: c * 1024 + ec * 128 + 128],
                                wq_lo[:, c * 1024 + ec * 128: c * 1024 + ec * 128 + 128],
                                xTb_hi[:, c * 256:(c + 1) * 256],
                                xTb_lo[:, c * 256:(c + 1) * 256],
                                c == 0, c == 7,
                                ev_xtev + [wb_ev[0][c // 2], bfree[bk_i]], want=(c == 7))
                ev_t1 = Sd.ve(lambda e, ec=ec, par=par, bk_i=bk_i: e.tensor_scalar(
                    out=t1c[:, par * 256:(par + 1) * 256], in0=pb[bk_i][:, 0:256],
                    scalar1=bq_col[:, ec:ec + 1], scalar2=0.0, op0=ALU.add, op1=ALU.min),
                    waits=[ev_q, ev_bq, qtf_free[par]])
                ev_t2 = Sd.se(lambda e, par=par: e.activation(
                    out=t2c[:, par * 256:(par + 1) * 256], in_=t1c[:, par * 256:(par + 1) * 256],
                    func=AF.Exp), waits=[ev_t1])
                ev_t3 = Sd.ve(lambda e, ec=ec, par=par, bk_i=bk_i: e.tensor_scalar(
                    out=qtf[:, par * 256:(par + 1) * 256], in0=pb[bk_i][:, 0:256],
                    scalar1=bq_col[:, ec:ec + 1], scalar2=0.0, op0=ALU.add, op1=ALU.max),
                    waits=[ev_q, qtf_free[par]])
                bfree[bk_i] = ev_t3
                ev_q4 = Sd.ve(lambda e, par=par: e.tensor_tensor(
                    out=qtf[:, par * 256:(par + 1) * 256], in0=qtf[:, par * 256:(par + 1) * 256],
                    in1=t2c[:, par * 256:(par + 1) * 256], op=ALU.add),
                    waits=[ev_t2, ev_t3])
                ev_qh = Sd.se(lambda e, ec=ec, par=par: e.copy(
                    out=qT_hi[:, ec * 256:(ec + 1) * 256], in_=qtf[:, par * 256:(par + 1) * 256]),
                    waits=[ev_q4, qT_cons])
                ev_ql = Sd.ve(lambda e, ec=ec, par=par: e.tensor_tensor(
                    out=qT_lo[:, ec * 256:(ec + 1) * 256], in0=qtf[:, par * 256:(par + 1) * 256],
                    in1=qT_hi[:, ec * 256:(ec + 1) * 256], op=ALU.subtract),
                    waits=[ev_qh, qT_cons])
                qtf_free[par] = ev_ql
                qT_ready.append(ev_ql)

            gs_ready = []
            last_g_mm = None
            for ec in range(8):
                bk_i = 4 + ec % 2
                for c in range(8):
                    last_g_mm = mm3s(pb[bk_i][:, 0:256],
                                     wg_hi[:, c * 1024 + ec * 128: c * 1024 + ec * 128 + 128],
                                     wg_lo[:, c * 1024 + ec * 128: c * 1024 + ec * 128 + 128],
                                     xTb_hi[:, c * 256:(c + 1) * 256],
                                     xTb_lo[:, c * 256:(c + 1) * 256],
                                     c == 0, c == 7,
                                     ev_xtev + [wb_ev[1][c // 2], bfree[bk_i]], want=(c == 7))
                ev_s = Sd.se(lambda e, ec=ec, bk_i=bk_i: e.activation(
                    out=gsb[:, ec * 256:(ec + 1) * 256], in_=pb[bk_i][:, 0:256],
                    func=AF.Sigmoid, bias=bg_col[:, ec:ec + 1]),
                    waits=[last_g_mm, ev_bg, gs_cons])
                bfree[bk_i] = ev_s
                gs_ready.append(ev_s)
            xTb_cons = last_g_mm

            ga_ready = []
            for pr in range(8):
                bk_i = 6 + pr % 2
                par = pr % 2
                ev_a = mm3s(pb[bk_i][:, 0:256],
                            kvbd_hi[:, pr * 128:(pr + 1) * 128],
                            kvbd_lo[:, pr * 128:(pr + 1) * 128],
                            qT_hi[:, pr * 256:(pr + 1) * 256],
                            qT_lo[:, pr * 256:(pr + 1) * 256],
                            True, True,
                            [qT_ready[pr], kv_ev, bfree[bk_i]])
                ev_m = Sd.ve(lambda e, pr=pr, par=par, bk_i=bk_i: e.tensor_tensor(
                    out=gaf[:, par * 256:(par + 1) * 256], in0=gsb[:, pr * 256:(pr + 1) * 256],
                    in1=pb[bk_i][:, 0:256], op=ALU.mult),
                    waits=[ev_a, gs_ready[pr], gaf_free[par]])
                ev_gh = Sd.se(lambda e, pr=pr, par=par: e.copy(
                    out=ga_hi[:, pr * 256:(pr + 1) * 256], in_=gaf[:, par * 256:(par + 1) * 256]),
                    waits=[ev_m, ga_cons])
                ev_gl = Sd.ve(lambda e, pr=pr, par=par: e.tensor_tensor(
                    out=ga_lo[:, pr * 256:(pr + 1) * 256], in0=gaf[:, par * 256:(par + 1) * 256],
                    in1=ga_hi[:, pr * 256:(pr + 1) * 256], op=ALU.subtract),
                    waits=[ev_gh, ga_cons])
                bfree[bk_i] = ev_gl
                gaf_free[par] = ev_gl
                ga_ready.append(ev_gl)
            qT_cons = ga_ready[-1]

            for t128 in range(2):
                ob = [2 + t128 * 2, 3 + t128 * 2]
                o_stops = []
                for hf in range(2):
                    ev_o = None
                    for c in range(8):
                        ev_o = mm3s(pb[ob[hf]][:, :],
                                    ga_hi[:, c * 256 + t128 * 128: c * 256 + t128 * 128 + 128],
                                    ga_lo[:, c * 256 + t128 * 128: c * 256 + t128 * 128 + 128],
                                    wo_hi[:, c * 1024 + hf * 512: c * 1024 + (hf + 1) * 512],
                                    wo_lo[:, c * 1024 + hf * 512: c * 1024 + (hf + 1) * 512],
                                    c == 0, c == 7,
                                    ga_ready + [wb_ev[2][c // 2], bfree[ob[hf]]], want=(c == 7))
                    o_stops.append(ev_o)
                ga_cons = o_stops[-1]
                gs_cons = o_stops[-1]
                ev_r = []
                for hf in range(2):
                    ev_r.append(Sd.ve(lambda e, hf=hf, ob=ob: e.scalar_tensor_tensor(
                        out=res1[:, hf * 512:(hf + 1) * 512], in0=pb[ob[hf]][:, :],
                        scalar=1.0, in1=bob[:, hf * 512:(hf + 1) * 512],
                        op0=ALU.mult, op1=ALU.add),
                        waits=[o_stops[hf], ev_bob] + sq_guard))
                    bfree[ob[hf]] = ev_r[hf]
                ev_r2 = Sd.ve(lambda e, p=p, t128=t128: e.tensor_tensor(
                    out=res1, in0=res1, in1=xb[p][:, t128 * 1024:(t128 + 1) * 1024],
                    op=ALU.add), waits=ev_r + [xb_ready[p]])
                if t128 == 1:
                    xb_free[p] = [ev_r2]
                ev_s1 = Sd.ve(lambda e: e.reduce_sum(out=ssum, in_=res1, axis=AX.X),
                              waits=[ev_r2])
                ev_nm = Sd.ve(lambda e: e.tensor_scalar_mul(out=negm, in0=ssum, scalar1=-1.0 / D),
                              waits=[ev_s1])
                ev_sq = Sd.se(lambda e: e.activation(out=sqd, in_=res1, func=AF.Square,
                                                     accum_out=ss2),
                              waits=[ev_r2] + sq_guard)
                ev_m2 = Sd.ve(lambda e: e.tensor_tensor(out=m2v, in0=negm, in1=negm, op=ALU.mult),
                              waits=[ev_nm])
                ev_var = Sd.ve(lambda e: e.tensor_scalar(
                    out=varv, in0=ss2, scalar1=1.0 / D, scalar2=m2v, op0=ALU.mult,
                    op1=ALU.subtract), waits=[ev_sq, ev_m2])
                ev_sv = Sd.se(lambda e: e.activation(out=svv, in_=varv, func=AF.Sqrt,
                                                     bias=epsc), waits=[ev_var, ev_eps])
                ev_rs = Sd.ve(lambda e: e.reciprocal(out=rstd, in_=svv), waits=[ev_sv])
                ev_h = Sd.ve(lambda e: e.tensor_scalar(
                    out=sqd, in0=res1, scalar1=negm, scalar2=rstd, op0=ALU.add, op1=ALU.mult),
                    waits=[ev_rs, ev_sq])
                Sd.dma("gp", lambda e, cc=cc, t128=t128: e.dma_start(
                    out=h_scr[cc * 256 + t128 * 128: cc * 256 + (t128 + 1) * 128, :],
                    in_=sqd), waits=[ev_h], inc="HW")
                hw_now = ("HW", Sd.cnt["HW"])
                htr_evs = []
                for cg in range(2):
                    bi = cg
                    ev_tr = None
                    for j in range(4):
                        c = cg * 4 + j
                        ev_tr = Sd.pe(lambda e, c=c, j=j, bi=bi: e.transpose(
                            pb[bi][:, j * 128:(j + 1) * 128],
                            sqd[:, c * 128:(c + 1) * 128], ident),
                            waits=[ev_h, bfree[bi]], want=(j == 3))
                    hi3 = hTb_hi.rearrange("q (c t) -> q c t", c=8)[:, cg * 4:(cg + 1) * 4, t128 * 128:(t128 + 1) * 128]
                    lo3 = hTb_lo.rearrange("q (c t) -> q c t", c=8)[:, cg * 4:(cg + 1) * 4, t128 * 128:(t128 + 1) * 128]
                    ps3 = pb[bi].rearrange("q (j t) -> q j t", j=4)[:, :, :]
                    ev_hhi = Sd.se(lambda e, hi3=hi3, ps3=ps3: e.copy(out=hi3, in_=ps3),
                                   waits=[ev_tr, hTb_guard])
                    ev_hlo = Sd.ve(lambda e, lo3=lo3, ps3=ps3, hi3=hi3: e.tensor_tensor(
                        out=lo3, in0=ps3, in1=hi3, op=ALU.subtract),
                        waits=[ev_hhi, hTb_guard])
                    bfree[bi] = ev_hlo
                    htr_evs.append(ev_hlo)
                sq_guard = [htr_evs[-1], hw_now]
            for c in range(8):
                Sd.dma("gp", lambda e, c=c, cc=cc: e.dma_start(
                    out=hth_scr[c, :, cc * 256:(cc + 1) * 256],
                    in_=hTb_hi[:, c * 256:(c + 1) * 256]),
                    waits=[sq_guard[0]], inc="HTW")
                hTb_guard = Sd.dma("gp", lambda e, c=c, cc=cc: e.dma_start(
                    out=htl_scr[c, :, cc * 256:(cc + 1) * 256],
                    in_=hTb_lo[:, c * 256:(c + 1) * 256]),
                    inc="HTW")

        barB = [("PE", Sd.cnt["PE"]), ("VE", Sd.cnt["VE"]), ("SE", Sd.cnt["SE"]),
                ("HW", Sd.cnt["HW"]), ("HTW", Sd.cnt["HTW"])]
        hw_total = ("HW", Sd.cnt["HW"])

        hTm_hi = PA[:, 0:4096].bitcast(BF16)
        hTm_lo = PA[:, 4096:8192].bitcast(BF16)
        acc = PA[:, 8192:16384]
        cbuf = [PM[:, 144:1168], PM[:, 1168:2192]]
        aTf = PM[:, 2192:2704]
        sh1b = PM[:, 2704:3216]
        aT_hi = PM[:, 3216:4240].bitcast(BF16)
        aT_lo = PM[:, 4240:5264].bitcast(BF16)
        h3cb = PM[:, 5264:5776]
        res2t = PM[:, 144:1168]
        sqd2 = PM[:, 1168:2192]
        outt = PM[:, 2192:3216]
        hln = PS[:, 2048:3072]
        cT = PS[0:16, 3072:4096]
        csc = PM[:, 5264:5392]
        eLb = PM[:, 5392:5408]
        gpad = PM[:, 5408:5416]
        g8b = PM[:, 5416:5424]
        gm4 = PM[:, 5424:5428]
        gex = PM[:, 5428:5444]
        mskb = PM[:, 5444:5460]
        m8b = PM[:, 5460:5468]
        selb = PM[:, 5468:5484]
        mst = PM[:, 5920:5936]
        mxs = mst[:, 0:1]
        nmx = mst[:, 1:2]
        sums = mst[:, 2:3]
        rs_ = mst[:, 3:4]
        ssum2 = mst[:, 4:5]
        negm2 = mst[:, 5:6]
        ss22 = mst[:, 6:7]
        m22 = mst[:, 7:8]
        var2 = mst[:, 8:9]
        sv2 = mst[:, 9:10]
        rstd2 = mst[:, 10:11]

        def slot_hi(kind, par):
            base = kind * 8192 + par * 4096
            return PW[:, base:base + 2048].bitcast(BF16)

        def slot_lo(kind, par):
            base = kind * 8192 + par * 4096
            return PW[:, base + 2048:base + 4096].bitcast(BF16)

        shared_hi = [PW[:, k * 8192:k * 8192 + 4096].bitcast(BF16) for k in range(3)]
        shared_lo = [PW[:, k * 8192 + 4096:(k + 1) * 8192].bitcast(BF16) for k in range(3)]

        half_end_bar = list(barB)
        sh_h_last = None
        hln_free = None
        res2_guard = None
        out_guard = None
        sh1_guard = None
        h3c_guard = None
        aTf_guard = None
        aT_cons = None
        slot_free = [[barB[0], barB[0]], [barB[0], barB[0]], [barB[0], barB[0]]]
        slot_rdy = [[None, None], [None, None], [None, None]]
        cb_ready = [None, None]
        stage_free_m = stage_free_b if isinstance(stage_free_b, list) else [stage_free_b]

        def load_split_slot(kind, par, e_idx):
            nonlocal stage_free_m
            hbm = (w1_h, w3_h, w2_h)[kind]
            rows = 512 if kind != 2 else 256
            nchunk = 4 if kind != 2 else 2
            ev_lo = None
            for half in range(2):
                ev_l = Sd.dma("sy", lambda e, hbm=hbm, e_idx=e_idx, rows=rows, nchunk=nchunk, half=half: e.dma_start(
                    out=wstage.rearrange("p (c i) -> p c i", c=nchunk),
                    in_=hbm[e_idx][half * rows:(half + 1) * rows].rearrange("(c p) i -> p c i", p=128)),
                    waits=stage_free_m, inc="SWL")
                hi_ap = slot_hi(kind, par)[:, half * 2048:(half + 1) * 2048]
                lo_ap = slot_lo(kind, par)[:, half * 2048:(half + 1) * 2048]
                ev_hi = Sd.dma("gp", lambda e, hi_ap=hi_ap: e.dma_start(
                    out=hi_ap, in_=wstage), waits=[ev_l, slot_free[kind][par]], inc="GPD")
                ev_lo = Sd.add("gp", lambda e, hi_ap=hi_ap, lo_ap=lo_ap: e.tensor_tensor(
                    out=lo_ap, in0=wstage, in1=hi_ap, op=ALU.subtract),
                    waits=[ev_hi], inc="GP")
                stage_free_m = [ev_lo]
            return ev_lo

        for hh in range(2):
            pe_gate = list(barB) if hh == 0 else [sh_h_last]
            ve_gate = list(barB) if hh == 0 else [("PE", Sd.cnt["PE"]), ("VE", Sd.cnt["VE"])]
            out_gate = None if hh == 0 else ("OUT", Sd.cnt["OUT"])
            ev_ht = None
            for c in range(8):
                Sd.dma("sy", lambda e, c=c, hh=hh: e.dma_start(
                    out=hTm_hi[:, c * 1024:(c + 1) * 1024],
                    in_=hth_scr[c, :, hh * 1024:(hh + 1) * 1024]),
                    waits=pe_gate, inc="HT")
                ev_ht = Sd.dma("sy", lambda e, c=c, hh=hh: e.dma_start(
                    out=hTm_lo[:, c * 1024:(c + 1) * 1024],
                    in_=htl_scr[c, :, hh * 1024:(hh + 1) * 1024]),
                    inc="HT")
            ev_acc0 = Sd.ve(lambda e: e.memset(acc, 0.0), waits=ve_gate)

            Lbuf = PM[:, 5264:5392]
            Ebuf = PM[:, 5392:5520]
            Mbuf = PS[:, 3584:3712]
            Mb2 = PS[:, 3712:3840]
            EQ128 = PS[:, 3840:3968]
            EQ32 = PS[:, 3968:4000]
            g32 = PS[:, 4000:4032]
            mx8 = PS[:, 4032:4040]
            m1b = PS[:, 4040:4048]
            thr2b = PS[:, 4048:4056]
            sumsb = PS[:, 4056:4064]
            rsb = PS[:, 4064:4072]
            thr4b = PS[:, 4072:4080]
            BIGC = 1e30

            ev_Ls = []
            for st in range(8):
                lb = 0 if st % 2 == 0 else 2
                ev_l = None
                for c in range(8):
                    sh = hTm_hi[:, c * 1024 + st * 128: c * 1024 + st * 128 + 128]
                    sl = hTm_lo[:, c * 1024 + st * 128: c * 1024 + st * 128 + 128]
                    gh_ = gwT_hi[:, c * 16:(c + 1) * 16]
                    gl_ = gwT_lo[:, c * 16:(c + 1) * 16]
                    w0 = [ev_ht, ev_gwT, bfree[lb]] + (pe_gate if st == 0 else [])
                    Sd.pe(lambda e, sh=sh, lb=lb, c=c, gh_=gh_: e.matmul(
                        pb[lb][:, 0:16], sh, gh_, start=(c == 0), stop=False), waits=w0,
                        want=False)
                    Sd.pe(lambda e, sh=sh, lb=lb, gl_=gl_: e.matmul(
                        pb[lb][:, 0:16], sh, gl_, start=False, stop=False), want=False)
                    Sd.pe(lambda e, sl=sl, lb=lb, gh_=gh_: e.matmul(
                        pb[lb][:, 0:16], sl, gh_, start=False, stop=False), want=False)
                    ev_l = Sd.pe(lambda e, sl=sl, lb=lb, c=c, gl_=gl_: e.matmul(
                        pb[lb][:, 0:16], sl, gl_, start=False, stop=(c == 7)),
                        want=(c == 7))
                ev_L = Sd.ve(lambda e, st=st, lb=lb: e.tensor_copy(
                    out=Lbuf[:, st * 16:(st + 1) * 16], in_=pb[lb][:, 0:16]),
                    waits=[ev_l] + (ve_gate if st == 0 else []))
                bfree[lb] = ev_L
                ev_Ls.append(ev_L)

            L3 = Lbuf.rearrange("p (s e) -> p s e", s=8)
            E3 = Ebuf.rearrange("p (s e) -> p s e", s=8)
            ev1 = Sd.ve(lambda e: e.tensor_reduce(out=mx8, in_=L3, axis=AX.X, op=ALU.max),
                        waits=ev_Ls)
            ev2 = Sd.ve(lambda e: e.tensor_tensor(
                out=E3, in0=L3,
                in1=mx8.unsqueeze(2).to_broadcast([128, 8, 16]),
                op=ALU.subtract), waits=[ev1])
            ev3 = Sd.se(lambda e: e.activation(out=Lbuf, in_=Ebuf, func=AF.Exp),
                        waits=[ev2])
            ev4 = Sd.ve(lambda e: e.tensor_reduce(out=sumsb, in_=L3, axis=AX.X, op=ALU.add),
                        waits=[ev3])
            ev5 = Sd.ve(lambda e: e.reciprocal(out=rsb, in_=sumsb), waits=[ev4])
            ev6 = Sd.ve(lambda e: e.tensor_tensor(
                out=E3, in0=L3,
                in1=rsb.unsqueeze(2).to_broadcast([128, 8, 16]),
                op=ALU.mult), waits=[ev5, ev3])
            Eg = Ebuf.rearrange("p (g j) -> p g j", j=4)
            ev7 = Sd.ve(lambda e: e.tensor_reduce(out=g32, in_=Eg, axis=AX.X, op=ALU.max),
                        waits=[ev6])
            G3 = g32.rearrange("p (s g) -> p s g", s=8)
            ev8 = Sd.ve(lambda e: e.tensor_reduce(out=m1b, in_=G3, axis=AX.X, op=ALU.max),
                        waits=[ev7])
            G32v = EQ32.rearrange("p (s g) -> p s g", s=8)
            ev9 = Sd.ve(lambda e: e.tensor_tensor(
                out=G32v, in0=G3,
                in1=m1b.unsqueeze(2).to_broadcast([128, 8, 4]),
                op=ALU.is_equal), waits=[ev8])
            ev10 = Sd.ve(lambda e: e.scalar_tensor_tensor(
                out=EQ32, in0=EQ32, scalar=-BIGC, in1=g32, op0=ALU.mult, op1=ALU.add),
                waits=[ev9])
            ev11 = Sd.ve(lambda e: e.tensor_reduce(
                out=thr2b, in_=EQ32.rearrange("p (s g) -> p s g", s=8), axis=AX.X,
                op=ALU.max), waits=[ev10])
            ev12 = Sd.ve(lambda e: e.tensor_tensor(
                out=G32v, in0=G3,
                in1=thr2b.unsqueeze(2).to_broadcast([128, 8, 4]),
                op=ALU.is_ge), waits=[ev11])
            Mg = Mbuf.rearrange("p (g j) -> p g j", j=4)
            ev13 = Sd.ve(lambda e: e.tensor_tensor(
                out=Mg, in0=Eg,
                in1=EQ32.unsqueeze(2).to_broadcast([128, 32, 4]),
                op=ALU.mult), waits=[ev12, ev6])
            ev14 = Sd.ve(lambda e: e.tensor_copy(out=Mb2, in_=Mbuf), waits=[ev13])
            M3 = Mb2.rearrange("p (s e) -> p s e", s=8)
            evp = ev14
            for r in range(4):
                evm = Sd.ve(lambda e: e.tensor_reduce(out=thr4b, in_=M3, axis=AX.X,
                                                      op=ALU.max), waits=[evp])
                if r == 3:
                    evp = evm
                    break
                eve = Sd.ve(lambda e: e.tensor_tensor(
                    out=EQ128.rearrange("p (s e) -> p s e", s=8), in0=M3,
                    in1=thr4b.unsqueeze(2).to_broadcast([128, 8, 16]),
                    op=ALU.is_equal), waits=[evm])
                evp = Sd.ve(lambda e: e.scalar_tensor_tensor(
                    out=Mb2, in0=EQ128, scalar=-BIGC, in1=Mb2, op0=ALU.mult, op1=ALU.add),
                    waits=[eve])
            ev15 = Sd.ve(lambda e: e.tensor_tensor(
                out=EQ128.rearrange("p (s e) -> p s e", s=8),
                in0=Mbuf.rearrange("p (s e) -> p s e", s=8),
                in1=thr4b.unsqueeze(2).to_broadcast([128, 8, 16]),
                op=ALU.is_ge), waits=[evp, ev13])
            ev_c = Sd.ve(lambda e: e.tensor_tensor(out=csc, in0=Ebuf, in1=EQ128,
                                                   op=ALU.mult), waits=[ev15, ev6])

            cT_evs = []
            for st in range(8):
                ev15t = Sd.pe(lambda e, st=st: e.transpose(
                    pb[1][0:16, 0:128], csc[:, st * 16:(st + 1) * 16], ident),
                    waits=[ev_c, bfree[1]])
                ev16t = Sd.ve(lambda e, st=st: e.tensor_copy(
                    out=cT[:, st * 128:(st + 1) * 128], in_=pb[1][0:16, 0:128]),
                    waits=[ev15t] + (ve_gate if st == 0 else []))
                bfree[1] = ev16t
                cT_evs.append(ev16t)

            for e_i in range(NE):
                par = e_i % 2
                for kind in range(3):
                    slot_rdy[kind][par] = load_split_slot(kind, par, e_i)
                cb_mm = []
                for hf in range(2):
                    cbk = 4 + hf
                    cb_mm.append(Sd.pe(lambda e, e_i=e_i, hf=hf, cbk=cbk: e.matmul(
                        pb[cbk][:, :], onehot[0:16, e_i * 128:(e_i + 1) * 128],
                        cT[:, hf * 512:(hf + 1) * 512], start=True, stop=True),
                        waits=cT_evs + [e_ldi, bfree[cbk]]))
                ev_cb = None
                for hf in range(2):
                    ev_cb = Sd.ve(lambda e, par=par, hf=hf, cbk=4 + hf: e.tensor_copy(
                        out=cbuf[par][:, hf * 512:(hf + 1) * 512], in_=pb[cbk][:, :]),
                        waits=[cb_mm[hf], stage2_guard])
                    bfree[4 + hf] = ev_cb
                cb_ready[par] = ev_cb

                h1_last = None
                h3_last = None
                for tg in range(2):
                    a_evs = []
                    y_stop = {}
                    YB = {0: 2, 1: 4, 2: 6, 3: 2}

                    def y_partial(j, t128s, tg=tg, par=par, a_evs=a_evs, y_stop=y_stop, YB=YB):
                        for t128 in t128s:
                            for hf in range(2):
                                bidx = YB[t128] + hf
                                sh = aT_hi[:, j * 512 + t128 * 128: j * 512 + t128 * 128 + 128]
                                sl = aT_lo[:, j * 512 + t128 * 128: j * 512 + t128 * 128 + 128]
                                mh = slot_hi(2, par)[:, j * 1024 + hf * 512: j * 1024 + (hf + 1) * 512]
                                ml = slot_lo(2, par)[:, j * 1024 + hf * 512: j * 1024 + (hf + 1) * 512]
                                ev_y = mm3s(pb[bidx][:, :], sh, sl, mh, ml,
                                            j == 0, j == 3,
                                            [a_evs[j], slot_rdy[2][par]]
                                            + ([bfree[bidx]] if j == 0 else []),
                                            want=(j == 3))
                                if j == 3:
                                    y_stop[(t128, hf)] = ev_y

                    def y_evac(t128s, tg=tg, y_stop=y_stop, YB=YB):
                        for t128 in t128s:
                            for hf in range(2):
                                bidx = YB[t128] + hf
                                stl = tg * 4 + t128
                                ev_acc = Sd.ve(lambda e, stl=stl, hf=hf, bidx=bidx: e.scalar_tensor_tensor(
                                    out=acc[:, stl * 1024 + hf * 512: stl * 1024 + (hf + 1) * 512],
                                    in0=pb[bidx][:, :], scalar=1.0,
                                    in1=acc[:, stl * 1024 + hf * 512: stl * 1024 + (hf + 1) * 512],
                                    op0=ALU.mult, op1=ALU.add),
                                    waits=[y_stop[(t128, hf)], ev_acc0])
                                bfree[bidx] = ev_acc

                    for ic in range(4):
                        ev_h1 = None
                        ev_h3 = None
                        for c in range(8):
                            mh = hTm_hi[:, c * 1024 + tg * 512: c * 1024 + tg * 512 + 512]
                            ml = hTm_lo[:, c * 1024 + tg * 512: c * 1024 + tg * 512 + 512]
                            s1h = slot_hi(0, par)[:, c * 512 + ic * 128: c * 512 + ic * 128 + 128]
                            s1l = slot_lo(0, par)[:, c * 512 + ic * 128: c * 512 + ic * 128 + 128]
                            ev_h1 = mm3s(pb[0][:, :], s1h, s1l, mh, ml, c == 0, c == 7,
                                         [slot_rdy[0][par], ev_ht, bfree[0]], want=(c == 7))
                        for c in range(8):
                            mh = hTm_hi[:, c * 1024 + tg * 512: c * 1024 + tg * 512 + 512]
                            ml = hTm_lo[:, c * 1024 + tg * 512: c * 1024 + tg * 512 + 512]
                            s3h = slot_hi(1, par)[:, c * 512 + ic * 128: c * 512 + ic * 128 + 128]
                            s3l = slot_lo(1, par)[:, c * 512 + ic * 128: c * 512 + ic * 128 + 128]
                            ev_h3 = mm3s(pb[1][:, :], s3h, s3l, mh, ml, c == 0, c == 7,
                                         [slot_rdy[1][par], ev_ht, bfree[1]], want=(c == 7))
                        h1_last = ev_h1
                        h3_last = ev_h3
                        ev_sh = Sd.se(lambda e: e.activation(
                            out=sh1b, in_=pb[0][:, :], func=AF.Silu),
                            waits=[ev_h1, sh1_guard, out_gate])
                        bfree[0] = ev_sh
                        ev_h3c = Sd.ve(lambda e, par=par, tg=tg: e.tensor_tensor(
                            out=h3cb, in0=pb[1][:, :],
                            in1=cbuf[par][:, tg * 512:(tg + 1) * 512], op=ALU.mult),
                            waits=[ev_h3, cb_ready[par], h3c_guard])
                        bfree[1] = ev_h3c
                        ev_a = Sd.ve(lambda e: e.tensor_tensor(
                            out=aTf, in0=sh1b, in1=h3cb,
                            op=ALU.mult), waits=[ev_sh, ev_h3c, aT_cons, aTf_guard])
                        ev_ahi = Sd.se(lambda e, ic=ic: e.copy(
                            out=aT_hi[:, ic * 512:(ic + 1) * 512],
                            in_=aTf),
                            waits=[ev_a, aT_cons])
                        ev_alo = Sd.ve(lambda e, ic=ic: e.tensor_tensor(
                            out=aT_lo[:, ic * 512:(ic + 1) * 512],
                            in0=aTf,
                            in1=aT_hi[:, ic * 512:(ic + 1) * 512], op=ALU.subtract),
                            waits=[ev_ahi, aT_cons])
                        sh1_guard = ev_a
                        h3c_guard = ev_a
                        aTf_guard = ev_alo
                        a_evs.append(ev_alo)
                        if ic >= 1:
                            y_partial(ic - 1, (0, 1, 2))
                    y_partial(3, (0, 1, 2))
                    y_evac((0,))
                    for j in range(4):
                        y_partial(j, (3,))
                    y_evac((1, 2, 3))
                    aT_cons = y_stop[(3, 1)]
                slot_free[0][par] = h1_last
                slot_free[1][par] = h3_last
                slot_free[2][par] = aT_cons

            sh_rdy = []
            for k in range(3):
                hbm = (ws1_h, ws3_h, ws2_h)[k]
                ev_lo = None
                for half in range(4):
                    ev_l = Sd.dma("sy", lambda e, hbm=hbm, half=half: e.dma_start(
                        out=wstage.rearrange("p (c i) -> p c i", c=2),
                        in_=hbm[half * 256:(half + 1) * 256].rearrange("(c p) i -> p c i", p=128)),
                        waits=stage_free_m, inc="WSH")
                    hi_ap = shared_hi[k][:, half * 2048:(half + 1) * 2048]
                    lo_ap = shared_lo[k][:, half * 2048:(half + 1) * 2048]
                    ev_hi = Sd.dma("gp", lambda e, hi_ap=hi_ap: e.dma_start(
                        out=hi_ap, in_=wstage),
                        waits=[ev_l, slot_free[k][0]], inc="GPD")
                    ev_lo = Sd.add("gp", lambda e, hi_ap=hi_ap, lo_ap=lo_ap: e.tensor_tensor(
                        out=lo_ap, in0=wstage, in1=hi_ap, op=ALU.subtract),
                        waits=[ev_hi, slot_free[k][1]], inc="GP")
                    stage_free_m = [ev_lo]
                sh_rdy.append(ev_lo)

            for tg in range(2):
                a_evs = []
                for ic in range(8):
                    b1i = 0 if ic % 2 == 0 else 2
                    b3i = 1 if ic % 2 == 0 else 3
                    ev_h1 = None
                    ev_h3 = None
                    for c in range(8):
                        mh = hTm_hi[:, c * 1024 + tg * 512: c * 1024 + tg * 512 + 512]
                        ml = hTm_lo[:, c * 1024 + tg * 512: c * 1024 + tg * 512 + 512]
                        s1h = shared_hi[0][:, c * 1024 + ic * 128: c * 1024 + ic * 128 + 128]
                        s1l = shared_lo[0][:, c * 1024 + ic * 128: c * 1024 + ic * 128 + 128]
                        ev_h1 = mm3s(pb[b1i][:, :], s1h, s1l, mh, ml, c == 0, c == 7,
                                     [sh_rdy[0], ev_ht, bfree[b1i]], want=(c == 7))
                    for c in range(8):
                        mh = hTm_hi[:, c * 1024 + tg * 512: c * 1024 + tg * 512 + 512]
                        ml = hTm_lo[:, c * 1024 + tg * 512: c * 1024 + tg * 512 + 512]
                        s3h = shared_hi[1][:, c * 1024 + ic * 128: c * 1024 + ic * 128 + 128]
                        s3l = shared_lo[1][:, c * 1024 + ic * 128: c * 1024 + ic * 128 + 128]
                        ev_h3 = mm3s(pb[b3i][:, :], s3h, s3l, mh, ml, c == 0, c == 7,
                                     [sh_rdy[1], ev_ht, bfree[b3i]], want=(c == 7))
                    sh_h_last = ev_h3
                    if ic == 4:
                        icg = 0
                        for t128 in range(4):
                            yb0 = 4 + ((t128 + icg) % 2) * 2
                            for hf in range(2):
                                bidx = yb0 + hf
                                ev_y = None
                                for ic2 in range(4):
                                    sh = aT_hi[:, ic2 * 512 + t128 * 128: ic2 * 512 + t128 * 128 + 128]
                                    sl = aT_lo[:, ic2 * 512 + t128 * 128: ic2 * 512 + t128 * 128 + 128]
                                    icc = icg * 4 + ic2
                                    mh = shared_hi[2][:, icc * 1024 + hf * 512: icc * 1024 + (hf + 1) * 512]
                                    ml = shared_lo[2][:, icc * 1024 + hf * 512: icc * 1024 + (hf + 1) * 512]
                                    ev_y = mm3s(pb[bidx][:, :], sh, sl, mh, ml,
                                                ic2 == 0, ic2 == 3,
                                                a_evs[icg * 4:] + [sh_rdy[2], bfree[bidx]],
                                                want=(ic2 == 3))
                                stl = tg * 4 + t128
                                ev_acc = Sd.ve(lambda e, stl=stl, hf=hf, bidx=bidx: e.scalar_tensor_tensor(
                                    out=acc[:, stl * 1024 + hf * 512: stl * 1024 + (hf + 1) * 512],
                                    in0=pb[bidx][:, :], scalar=1.0,
                                    in1=acc[:, stl * 1024 + hf * 512: stl * 1024 + (hf + 1) * 512],
                                    op0=ALU.mult, op1=ALU.add),
                                    waits=[ev_y, ev_acc0])
                                bfree[bidx] = ev_acc
                        aT_cons = ev_y
                    ev_sh = Sd.se(lambda e, b1i=b1i: e.activation(
                        out=sh1b, in_=pb[b1i][:, :], func=AF.Silu),
                        waits=[ev_h1, sh1_guard])
                    bfree[b1i] = ev_sh
                    ev_a = Sd.ve(lambda e, b3i=b3i: e.tensor_tensor(
                        out=aTf, in0=sh1b, in1=pb[b3i][:, :],
                        op=ALU.mult), waits=[ev_sh, ev_h3, aT_cons, h3c_guard, aTf_guard])
                    bfree[b3i] = ev_a
                    ev_ahi = Sd.se(lambda e, ic=ic: e.copy(
                        out=aT_hi[:, (ic % 4) * 512:((ic % 4) + 1) * 512],
                        in_=aTf),
                        waits=[ev_a, aT_cons])
                    ev_alo = Sd.ve(lambda e, ic=ic: e.tensor_tensor(
                        out=aT_lo[:, (ic % 4) * 512:((ic % 4) + 1) * 512],
                        in0=aTf,
                        in1=aT_hi[:, (ic % 4) * 512:((ic % 4) + 1) * 512], op=ALU.subtract),
                        waits=[ev_ahi, aT_cons])
                    sh1_guard = ev_a
                    h3c_guard = ev_a
                    aTf_guard = ev_alo
                    a_evs.append(ev_alo)
                    if ic == 7:
                        icg = 1
                        for t128 in range(4):
                            yb0 = 4 + ((t128 + icg) % 2) * 2
                            for hf in range(2):
                                bidx = yb0 + hf
                                ev_y = None
                                for ic2 in range(4):
                                    sh = aT_hi[:, ic2 * 512 + t128 * 128: ic2 * 512 + t128 * 128 + 128]
                                    sl = aT_lo[:, ic2 * 512 + t128 * 128: ic2 * 512 + t128 * 128 + 128]
                                    icc = icg * 4 + ic2
                                    mh = shared_hi[2][:, icc * 1024 + hf * 512: icc * 1024 + (hf + 1) * 512]
                                    ml = shared_lo[2][:, icc * 1024 + hf * 512: icc * 1024 + (hf + 1) * 512]
                                    ev_y = mm3s(pb[bidx][:, :], sh, sl, mh, ml,
                                                ic2 == 0, ic2 == 3,
                                                a_evs[icg * 4:] + [sh_rdy[2], bfree[bidx]],
                                                want=(ic2 == 3))
                                stl = tg * 4 + t128
                                ev_acc = Sd.ve(lambda e, stl=stl, hf=hf, bidx=bidx: e.scalar_tensor_tensor(
                                    out=acc[:, stl * 1024 + hf * 512: stl * 1024 + (hf + 1) * 512],
                                    in0=pb[bidx][:, :], scalar=1.0,
                                    in1=acc[:, stl * 1024 + hf * 512: stl * 1024 + (hf + 1) * 512],
                                    op0=ALU.mult, op1=ALU.add),
                                    waits=[ev_y, ev_acc0])
                                bfree[bidx] = ev_acc
                        aT_cons = ev_y

            ln_pe_wait = ("PE", Sd.cnt["PE"])
            for st in range(8):
                ev_hl = Sd.dma("se", lambda e, st=st, hh=hh: e.dma_start(
                    out=hln, in_=h_scr[hh * 1024 + st * 128: hh * 1024 + (st + 1) * 128, :]),
                    waits=[hln_free, hw_total, ln_pe_wait], inc="HR")
                ev_r2 = Sd.ve(lambda e, st=st: e.tensor_tensor(
                    out=res2t, in0=acc[:, st * 1024:(st + 1) * 1024], in1=hln, op=ALU.add),
                    waits=[ev_hl, bfree[4], bfree[5], bfree[6], bfree[7], aT_cons,
                           res2_guard])
                hln_free = ev_r2
                ev_s1 = Sd.ve(lambda e: e.reduce_sum(out=ssum2, in_=res2t, axis=AX.X),
                              waits=[ev_r2])
                ev_nm = Sd.ve(lambda e: e.tensor_scalar_mul(out=negm2, in0=ssum2,
                                                            scalar1=-1.0 / D), waits=[ev_s1])
                ev_sq = Sd.se(lambda e: e.activation(out=sqd2, in_=res2t, func=AF.Square,
                                                     accum_out=ss22),
                              waits=[ev_r2])
                ev_m2 = Sd.ve(lambda e: e.tensor_tensor(out=m22, in0=negm2, in1=negm2,
                                                        op=ALU.mult), waits=[ev_nm])
                ev_var = Sd.ve(lambda e: e.tensor_scalar(
                    out=var2, in0=ss22, scalar1=1.0 / D, scalar2=m22, op0=ALU.mult,
                    op1=ALU.subtract), waits=[ev_sq, ev_m2])
                ev_sv = Sd.se(lambda e: e.activation(out=sv2, in_=var2, func=AF.Sqrt,
                                                     bias=epsc), waits=[ev_var, ev_eps])
                ev_rs = Sd.ve(lambda e: e.reciprocal(out=rstd2, in_=sv2), waits=[ev_sv])
                ev_o = Sd.ve(lambda e: e.tensor_scalar(
                    out=outt, in0=res2t, scalar1=negm2, scalar2=rstd2, op0=ALU.add,
                    op1=ALU.mult), waits=[ev_rs, out_guard])
                res2_guard = ev_o
                out_guard = Sd.dma("gp", lambda e, st=st, hh=hh: e.dma_start(
                    out=out_h[hh * 1024 + st * 128: hh * 1024 + (st + 1) * 128, :],
                    in_=outt), waits=[ev_o], inc="OUT")

            half_end_bar = [("PE", Sd.cnt["PE"]), ("VE", Sd.cnt["VE"]),
                            ("SE", Sd.cnt["SE"]), ("OUT", Sd.cnt["OUT"])]

        out_total = Sd.cnt["OUT"]
        Sd.add("gp", lambda e: e.wait_ge(sems["OUT"], out_total))

        with nc.Block() as block:
            def mk(engname):
                def run(eng):
                    for fn, ws, inc, n in Sd.ops[engname]:
                        for s_, v_ in ws:
                            eng.wait_ge(sems[s_], v_)
                        ins = fn(eng)
                        if inc is not None:
                            ins.then_inc(sems[inc], n)
                return run
            block.tensor(mk("pe"))
            block.vector(mk("ve"))
            block.scalar(mk("se"))
            block.sync(mk("sy"))
            block.gpsimd(mk("gp"))
    return nc


_NC_CACHE = [None]


def _aux_array():
    a = np.zeros((128, 2304), dtype=np.float32)
    a[:, 0:128] = np.eye(128, dtype=np.float32)
    for e in range(NE):
        a[e, 128 + e * 128:128 + (e + 1) * 128] = 1.0
    a[0, 2176:2304] = 1.0
    return a


def kernel(L0_attn_x, L0_attn_Wq, L0_attn_bq, L0_attn_Wk, L0_attn_bk,
           L0_attn_Wv, L0_attn_bv, L0_attn_Wg, L0_attn_bg, L0_attn_Wo,
           L0_attn_bo, L0_ffn_gate_w, L0_ffn_w1, L0_ffn_w2, L0_ffn_w3,
           L0_ffn_ws1, L0_ffn_ws2, L0_ffn_ws3):
    x = np.ascontiguousarray(np.asarray(L0_attn_x, dtype=np.float32))
    c32 = lambda a: np.ascontiguousarray(np.asarray(a, dtype=np.float32))
    base = dict(Wq=c32(L0_attn_Wq), bq=c32(L0_attn_bq), Wk=c32(L0_attn_Wk),
                bk=c32(L0_attn_bk), Wv=c32(L0_attn_Wv), bv=c32(L0_attn_bv),
                Wg=c32(L0_attn_Wg), bg=c32(L0_attn_bg), Wo=c32(L0_attn_Wo),
                bo=c32(L0_attn_bo), gate_w=c32(L0_ffn_gate_w), w1=c32(L0_ffn_w1),
                w2=c32(L0_ffn_w2), w3=c32(L0_ffn_w3), ws1=c32(L0_ffn_ws1),
                ws2=c32(L0_ffn_ws2), ws3=c32(L0_ffn_ws3), aux=_aux_array())
    if _NC_CACHE[0] is None:
        _NC_CACHE[0] = build_program()
    nc = _NC_CACHE[0]
    in_maps = [dict(base, x=x[c]) for c in range(NCORE)]
    res = run_bass_kernel_spmd(nc, in_maps, core_ids=list(range(NCORE)))
    return np.stack([np.asarray(res.results[c]["out"]) for c in range(NCORE)], axis=0)
